# version 01 (final)
# baseline (speedup 1.0000x reference)
import math
import contextlib
import numpy as np
import ml_dtypes
import concourse.bass as bass
import concourse.mybir as mybir
from concourse.bass_utils import run_bass_kernel_spmd

F32 = mybir.dt.float32
BF16 = mybir.dt.bfloat16
I32 = mybir.dt.int32
AF = mybir.ActivationFunctionType
ALU = mybir.AluOpType
AX = mybir.AxisListType
IOX = bass.IndirectOffsetOnAxis

NCORES = 8
ROWS = 4
T = 2048
D = 1024
H = 16
HD = 64
DFF = 4096
NEG = -1e30
NTILE = ROWS * T // 128


def alibi_slopes(n_heads):
    start = 2 ** (-2 ** (-(math.log2(n_heads) - 3)))
    return [start * start ** i for i in range(n_heads)]


def host_consts():
    c = {}
    c["ident"] = np.eye(128, dtype=ml_dtypes.bfloat16)
    i = np.arange(128)[:, None]
    cc = np.arange(T + 512)[None, :]
    M = (i + (T - 1) - cc).astype(np.float64)
    c["alibi_master"] = np.where(cc <= i + (T - 1), M, NEG).astype(np.float32)
    c["causal_master"] = np.where(cc <= i + (T - 1), 0.0, NEG).astype(np.float32)
    ii = np.arange(128)[:, None]
    jj = np.arange(128)[None, :]
    c["mask_far"] = np.where(ii >= jj, NEG, 0.0).astype(np.float32)
    c["ones_row"] = np.ones((1, 512), dtype=ml_dtypes.bfloat16)
    c["ones_f32"] = np.ones((1, 512), dtype=np.float32)
    c["iota8"] = np.arange(8, dtype=np.float32).reshape(1, 8)
    c["iota128"] = np.arange(128, dtype=np.float32).reshape(128, 1)
    c["ones_col"] = np.ones((128, 1), dtype=np.float32)
    c["ident_f32"] = np.eye(128, dtype=np.float32)
    pp = np.arange(128)
    g = pp % 32
    dt = pp // 32
    c["rowpat"] = ((g // 4) * 8192 + (g % 4) * 2048 + dt).astype(np.float32).reshape(128, 1)
    return c


_CONST_SHAPES = {
    "ident": ([128, 128], BF16),
    "alibi_master": ([128, T + 512], F32),
    "causal_master": ([128, T + 512], F32),
    "mask_far": ([128, 128], F32),
    "ones_row": ([1, 512], BF16),
    "ones_f32": ([1, 512], F32),
    "iota8": ([1, 8], F32),
    "iota128": ([128, 1], F32),
    "ones_col": ([128, 1], F32),
    "ident_f32": ([128, 128], F32),
    "rowpat": ([128, 1], F32),
}


class Sem:
    def __init__(self, h):
        self.h = h
        self.n = 0

    def inc(self, inst, k=1):
        inst.then_inc(self.h, k)
        self.n += k
        return inst


class Prog:
    def __init__(self):
        self.nc = bass.Bass("TRN2", target_bir_lowering=False)
        self.es = contextlib.ExitStack()
        self.block = None
        self.sems = {}

    def start(self):
        for nm in ("ld", "w", "t", "v", "s", "g", "st", "cc"):
            self.sems[nm] = Sem(self.es.enter_context(self.nc.semaphore("sem_" + nm)))
        return self

    def begin(self):
        self.block = self.es.enter_context(self.nc.Block())
        return self

    def sbuf(self, name, shape, dtype):
        return self.es.enter_context(self.nc.sbuf_tensor(name, shape, dtype))

    def psum(self, name, shape, dtype):
        return self.es.enter_context(self.nc.psum_tensor(name, shape, dtype))

    def close(self):
        self.es.close()


def declare_consts(p, used):
    aps = {}
    for k in used:
        shp, dt = _CONST_SHAPES[k]
        aps[k] = p.nc.dram_tensor("c_" + k, shp, dt, kind="ExternalInput").ap()
    return aps


def const_feed(used):
    c = host_consts()
    return {"c_" + k: np.asarray(c[k]) for k in used}


def emit_ln(p, x_sb, xn_sb, stat, scr, n=D, eps=1e-5):
    sv, ss = p.sems["v"], p.sems["s"]

    def v1(v):
        v.tensor_scalar(out=scr[:, 0:n], in0=x_sb[:, 0:n], scalar1=1.0, scalar2=None,
                        op0=ALU.mult, op1=ALU.add, accum_out=stat[:, 0:1])
        sv.inc(v.scalar_tensor_tensor(out=scr[:, 0:n], in0=x_sb[:, 0:n], scalar=1.0,
                               in1=x_sb[:, 0:n], op0=ALU.bypass, op1=ALU.mult,
                               accum_out=stat[:, 1:2]))
        v.wait_ge(sv.h, sv.n)
        sv.inc(v.tensor_scalar(out=stat[:, 2:3], in0=stat[:, 0:1], scalar1=1.0 / n,
                               scalar2=None, op0=ALU.mult))
        sv.inc(v.tensor_scalar(out=stat[:, 3:4], in0=stat[:, 1:2], scalar1=1.0 / n,
                               scalar2=None, op0=ALU.mult))
        v.wait_ge(sv.h, sv.n)
        sv.inc(v.tensor_tensor(out=stat[:, 5:6], in0=stat[:, 2:3], in1=stat[:, 2:3], op=ALU.mult))
        v.wait_ge(sv.h, sv.n)
        sv.inc(v.tensor_tensor(out=stat[:, 3:4], in0=stat[:, 3:4], in1=stat[:, 5:6], op=ALU.subtract))
        v.wait_ge(sv.h, sv.n)
        sv.inc(v.tensor_scalar(out=stat[:, 6:7], in0=stat[:, 3:4], scalar1=eps,
                               scalar2=None, op0=ALU.add))
    p.block.vector(v1)

    def s1(s):
        s.wait_ge(sv.h, sv.n)
        ss.inc(s.activation(out=stat[:, 7:8], in_=stat[:, 6:7], func=AF.Sqrt))
    p.block.scalar(s1)

    def v2(v):
        v.wait_ge(ss.h, ss.n)
        sv.inc(v.reciprocal(out=stat[:, 4:5], in_=stat[:, 7:8]))
        v.wait_ge(sv.h, sv.n)
        sv.inc(v.tensor_scalar(out=xn_sb[:, 0:n], in0=x_sb[:, 0:n], scalar1=stat[:, 2:3],
                               scalar2=stat[:, 4:5], op0=ALU.subtract, op1=ALU.mult))
    p.block.vector(v2)


def build_attn_a(kind):
    p = Prog()
    nc = p.nc
    x_in = nc.dram_tensor("x", [ROWS * T, D], F32, kind="ExternalInput").ap()
    x_out = nc.dram_tensor("y", [ROWS * T, D], F32, kind="ExternalOutput").ap()
    W = {n: nc.dram_tensor(n, [D, D], F32, kind="ExternalInput").ap()
         for n in ("Wq", "Wk", "Wv", "Wo")}
    B = {n: nc.dram_tensor(n, [1, D], F32, kind="ExternalInput").ap()
         for n in ("bq", "bk", "bv", "bo")}
    master_key = "alibi_master" if kind == "l0a" else "causal_master"
    consts = declare_consts(p, ["ident", master_key, "mask_far", "ones_row"])

    alibi = kind == "l0a"
    slopes = alibi_slopes(H)
    alpha = 1.0 / math.sqrt(HD)
    WIN = 256

    p.start()
    W3 = p.sbuf("W3", [128, 3, 8, D], BF16)
    Wo_sb = p.sbuf("Wo_sb", [128, 8, D], BF16)
    bbf = p.sbuf("bbf", [1, 4, D], BF16)
    ident = p.sbuf("ident", [128, 128], BF16)
    onesr = p.sbuf("onesr", [1, 512], BF16)
    mh = p.sbuf("mh", [128, T + 512], F32)
    maskfar = p.sbuf("maskfar", [128, 128], F32)
    xt = p.sbuf("xt", [128, 2, D], F32)
    xn = p.sbuf("xn", [128, D], BF16)
    stat = p.sbuf("stat", [128, 8], F32)
    xnT = p.sbuf("xnT", [128, 8, 512], BF16)
    qc = p.sbuf("qc", [128, 8, 512], BF16)
    kT = p.sbuf("kT", [128, 8, T], BF16)
    vr = p.sbuf("vr", [128, 16, D], BF16)
    ctxb = p.sbuf("ctxb", [128, 8, 512], BF16)
    Pb = p.sbuf("Pb", [128, 4, T], BF16)
    pT = p.sbuf("pT", [128, 2, 512], BF16)
    sgn = p.sbuf("sgn", [128, 8], F32)
    ysb = p.sbuf("ysb", [128, 2, D], F32)

    ps_big = p.psum("ps_big", [128, T], F32)
    ps_a = p.psum("ps_a", [128, 512], F32)
    ps_b = p.psum("ps_b", [128, 512], F32)
    ps_tr = p.psum("ps_tr", [128, 1024], BF16)
    ps_ctx = p.psum("ps_ctx", [64, 512], F32)

    s_ld, s_w, s_t, s_v, s_s, s_st = (p.sems[n] for n in ("ld", "w", "t", "v", "s", "st"))
    p.begin()

    def loads(g):
        for wi, wn in enumerate(("Wq", "Wk", "Wv")):
            for kc in range(8):
                s_w.inc(g.dma_start(out=W3[:, wi, kc, :], in_=W[wn][kc * 128:(kc + 1) * 128, :]), 16)
        for kc in range(8):
            s_w.inc(g.dma_start(out=Wo_sb[:, kc, :], in_=W["Wo"][kc * 128:(kc + 1) * 128, :]), 16)
        for bi, bn in enumerate(("bq", "bk", "bv", "bo")):
            s_w.inc(g.dma_start(out=bbf[0:1, bi, :], in_=B[bn][:, :]), 16)
        s_w.inc(g.dma_start(out=ident[:, :], in_=consts["ident"][:, :]), 16)
        s_w.inc(g.dma_start(out=onesr[:, :], in_=consts["ones_row"][:, :]), 16)
        s_w.inc(g.dma_start(out=maskfar[:, :], in_=consts["mask_far"][:, :]), 16)
        if not alibi:
            s_w.inc(g.dma_start(out=mh[:, :], in_=consts[master_key][:, :]), 16)
    p.block.gpsimd(loads)
    w_loaded = s_w.n

    for r in range(ROWS):
        base = r * T
        for c in range(4):
            t0c = c * 512
            for j in range(4):
                tile = base + t0c + j * 128
                buf = j % 2

                def ld_x(q, tile=tile, buf=buf):
                    q.wait_ge(s_v.h, s_v.n)
                    s_ld.inc(q.dma_start(out=xt[:, buf, :], in_=x_in[tile:tile + 128, :]), 16)
                p.block.sync(ld_x)
                ldn = s_ld.n

                def wv0(v, ldn=ldn):
                    v.wait_ge(s_ld.h, ldn)
                    v.wait_ge(s_w.h, w_loaded)
                p.block.vector(wv0)
                emit_ln(p, xt[:, buf, :], xn, stat, xn)

                def trx(t, j=j):
                    t.wait_ge(s_v.h, s_v.n)
                    for kc in range(8):
                        ins = t.transpose(ps_tr[:, kc * 128:(kc + 1) * 128],
                                          xn[:, kc * 128:(kc + 1) * 128], ident[:, :])
                    s_t.inc(ins)
                p.block.tensor(trx)

                def evx(v, j=j):
                    v.wait_ge(s_t.h, s_t.n)
                    s_v.inc(v.tensor_copy(
                        out=xnT[:, :, j * 128:(j + 1) * 128],
                        in_=ps_tr[:, :].rearrange("p (a b) -> p a b", a=8)))
                p.block.vector(evx)

            xnT_ready = s_v.n

            for wi, dst in ((0, "q"), (1, "k")):
                def projqk(t, wi=wi, xnT_ready=xnT_ready, sbase=s_s.n):
                    t.wait_ge(s_v.h, xnT_ready)
                    if wi == 1:
                        t.wait_ge(s_s.h, sbase)
                    for oc in range(8):
                        if oc >= 2:
                            t.wait_ge(s_s.h, sbase + oc - 1)
                        pp = ps_a if oc % 2 == 0 else ps_b
                        for kc in range(8):
                            t.matmul(pp[:, :], W3[:, wi, kc, oc * 128:(oc + 1) * 128],
                                     xnT[:, kc, :], start=(kc == 0), stop=False)
                        ins = t.matmul(pp[:, :], bbf[0:1, wi, oc * 128:(oc + 1) * 128],
                                       onesr[0:1, :], start=False, stop=True)
                        s_t.inc(ins)
                p.block.tensor(projqk)
                tq = s_t.n

                def evqk(s, dst=dst, c=c, tq=tq):
                    for oc in range(8):
                        s.wait_ge(s_t.h, tq - 8 + oc + 1)
                        pp = ps_a if oc % 2 == 0 else ps_b
                        if dst == "q":
                            out = qc[:, oc, :]
                        else:
                            out = kT[:, oc, t0c:t0c + 512]
                        s_s.inc(s.copy(out=out, in_=pp[:, :]))
                p.block.scalar(evqk)

            def projv(t, c=c, sbase=s_s.n):
                t.wait_ge(s_s.h, sbase)
                for j in range(4):
                    for half in range(2):
                        idx = j * 2 + half
                        if idx >= 2:
                            t.wait_ge(s_s.h, sbase + idx - 1)
                        pp = ps_a if idx % 2 == 0 else ps_b
                        for kc in range(8):
                            t.matmul(pp[:, :], xnT[:, kc, j * 128:(j + 1) * 128],
                                     W3[:, 2, kc, half * 512:(half + 1) * 512],
                                     start=(kc == 0), stop=False)
                        ins = t.matmul(pp[:, :], onesr[0:1, 0:128],
                                       bbf[0:1, 2, half * 512:(half + 1) * 512],
                                       start=False, stop=True)
                        s_t.inc(ins)
            p.block.tensor(projv)
            tv = s_t.n

            def evv(s, c=c, tv=tv):
                for j in range(4):
                    for half in range(2):
                        idx = j * 2 + half
                        s.wait_ge(s_t.h, tv - 8 + idx + 1)
                        pp = ps_a if idx % 2 == 0 else ps_b
                        s_s.inc(s.copy(out=vr[:, c * 4 + j, half * 512:(half + 1) * 512],
                                       in_=pp[:, :]))
            p.block.scalar(evv)

            span = t0c + 512
            nsb = span // 128
            for h in range(H):
                hb = (h % 2) * 64
                hc = h // 2
                if alibi:
                    def mkmh(g, h=h):
                        g.wait_ge(s_s.h, s_s.n)
                        s_w.inc(g.dma_start(out=mh[:, :], in_=consts[master_key][:, :]), 16)
                    p.block.gpsimd(mkmh)
                    mwn = s_w.n

                    def scmh(v, h=h, mwn=mwn):
                        v.wait_ge(s_w.h, mwn)
                        s_v.inc(v.tensor_scalar(out=mh[:, :], in0=mh[:, :],
                                                scalar1=slopes[h] / alpha, scalar2=None,
                                                op0=ALU.mult))
                    p.block.vector(scmh)
                mh_ready = s_v.n

                tile_infos = []
                for j in range(4):
                    t0 = t0c + j * 128
                    if kind == "l1a":
                        s_lo = max(0, t0 - 256)
                    else:
                        s_lo = 0
                    s_hi = t0 + 128
                    tile_infos.append((j, t0, s_lo, s_hi))

                for (j, t0, s_lo, s_hi) in tile_infos:
                    sspan = s_hi - s_lo

                    def scores(t, h=h, hb=hb, hc=hc, j=j, s_lo=s_lo, s_hi=s_hi,
                               mh_ready=mh_ready):
                        t.wait_ge(s_v.h, mh_ready)
                        t.wait_ge(s_s.h, s_s.n)
                        for sc in range(s_lo, s_hi, 512):
                            ww = min(512, s_hi - sc)
                            ins = t.matmul(ps_big[:, sc - s_lo:sc - s_lo + ww],
                                           qc[hb:hb + 64, hc, j * 128:(j + 1) * 128],
                                           kT[hb:hb + 64, hc, sc:sc + ww],
                                           start=True, stop=True)
                        s_t.inc(ins)
                    p.block.tensor(scores)

                    def soft_v(v, j=j, t0=t0, s_lo=s_lo, s_hi=s_hi):
                        v.wait_ge(s_t.h, s_t.n)
                        sp = s_hi - s_lo
                        if alibi:
                            c0 = (T - 1) - t0 + s_lo
                            s_v.inc(v.tensor_tensor(out=ps_big[:, 0:sp], in0=ps_big[:, 0:sp],
                                            in1=mh[:, c0:c0 + sp], op=ALU.add))
                        else:
                            s_v.inc(v.tensor_tensor(out=ps_big[:, sp - 128:sp],
                                            in0=ps_big[:, sp - 128:sp],
                                            in1=mh[:, (T - 1):(T - 1) + 128], op=ALU.add))
                            if s_hi - s_lo > 256:
                                s_v.inc(v.tensor_tensor(out=ps_big[:, 0:128],
                                                in0=ps_big[:, 0:128],
                                                in1=maskfar[:, :], op=ALU.add))
                        v.wait_ge(s_v.h, s_v.n)
                        s_v.inc(v.tensor_reduce(out=sgn[:, 0:1], in_=ps_big[:, 0:sp],
                                        axis=AX.X, op=ALU.max, negate=True))
                        v.wait_ge(s_v.h, s_v.n)
                        s_v.inc(v.tensor_scalar(out=sgn[:, 1:2], in0=sgn[:, 0:1],
                                                scalar1=alpha, scalar2=None, op0=ALU.mult))
                    p.block.vector(soft_v)

                    def soft_s(s, j=j, s_lo=s_lo, s_hi=s_hi):
                        s.wait_ge(s_v.h, s_v.n)
                        sp = s_hi - s_lo
                        s_s.inc(s.activation(out=Pb[:, j, 0:sp], in_=ps_big[:, 0:sp],
                                             func=AF.Exp, bias=sgn[:, 1:2], scale=alpha,
                                             accum_out=sgn[:, 2:3]))
                    p.block.scalar(soft_s)

                    def soft_n(v, j=j, s_lo=s_lo, s_hi=s_hi, span=span, t0c=t0c):
                        v.wait_ge(s_s.h, s_s.n)
                        sp = s_hi - s_lo
                        s_v.inc(v.reciprocal(out=sgn[:, 3:4], in_=sgn[:, 2:3]))
                        v.wait_ge(s_v.h, s_v.n)
                        last = v.tensor_scalar(out=Pb[:, j, 0:sp], in0=Pb[:, j, 0:sp],
                                               scalar1=sgn[:, 3:4], scalar2=None, op0=ALU.mult)
                        if kind == "l0a" and sp < span:
                            last = v.memset(Pb[:, j, sp:span], 0.0)
                        s_v.inc(last)
                    p.block.vector(soft_n)

                if kind == "l0a":
                    V0 = s_v.n
                    T0 = s_t.n

                    def pvt(t, h=h, nsb=nsb, V0=V0):
                        t.wait_ge(s_v.h, V0)
                        last = None
                        for sb in range(nsb):
                            for j in range(4):
                                ins = t.transpose(ps_tr[:, j * 128:(j + 1) * 128],
                                                  Pb[:, j, sb * 128:(sb + 1) * 128],
                                                  ident[:, :])
                            ins.then_inc(s_t.h, 1)
                            s_t.n += 1
                            t.wait_ge(s_v.h, V0 + sb + 1)
                            last = t.matmul(ps_ctx[:, :], vr[:, sb, h * 64:h * 64 + 64],
                                            pT[:, sb % 2, :],
                                            start=(sb == 0), stop=(sb == nsb - 1))
                        s_t.inc(last)
                    p.block.tensor(pvt)

                    def pve(v, nsb=nsb, T0=T0):
                        for sb in range(nsb):
                            v.wait_ge(s_t.h, T0 + sb + 1)
                            s_v.inc(v.tensor_copy(out=pT[:, sb % 2, :], in_=ps_tr[:, 0:512]))
                    p.block.vector(pve)
                else:
                    V0 = s_v.n
                    T0 = s_t.n
                    infos = tile_infos

                    def pvt1(t, h=h, infos=infos, V0=V0):
                        last = None
                        ev = 0
                        for (j, t0, s_lo, s_hi) in infos:
                            nblk = (s_hi - s_lo) // 128
                            t.wait_ge(s_v.h, V0 + ev)
                            for bi in range(nblk):
                                ins = t.transpose(ps_tr[:, bi * 128:(bi + 1) * 128],
                                                  Pb[:, j, bi * 128:(bi + 1) * 128],
                                                  ident[:, :])
                            ins.then_inc(s_t.h, 1)
                            s_t.n += 1
                            t.wait_ge(s_v.h, V0 + ev + 1)
                            ev += 1
                            for bi in range(nblk):
                                sb = (s_lo // 128) + bi
                                last = t.matmul(ps_ctx[:, j * 128:(j + 1) * 128],
                                                vr[:, sb, h * 64:h * 64 + 64],
                                                pT[:, 0, bi * 128:(bi + 1) * 128],
                                                start=(bi == 0), stop=(bi == nblk - 1))
                        s_t.inc(last)
                    p.block.tensor(pvt1)

                    def pve1(v, infos=infos, T0=T0):
                        ev = 0
                        for (j, t0, s_lo, s_hi) in infos:
                            nblk = (s_hi - s_lo) // 128
                            v.wait_ge(s_t.h, T0 + ev + 1)
                            ev += 1
                            s_v.inc(v.tensor_copy(out=pT[:, 0, 0:nblk * 128],
                                                  in_=ps_tr[:, 0:nblk * 128]))
                    p.block.vector(pve1)

                def evctx(s, h=h, hb=hb, hc=hc):
                    s.wait_ge(s_t.h, s_t.n)
                    s_s.inc(s.copy(out=ctxb[hb:hb + 64, hc, :], in_=ps_ctx[:, :]))
                p.block.scalar(evctx)

            ctx_ready = s_s.n
            for j in range(4):
                tile = base + t0c + j * 128
                buf = j % 2

                def ld_res(q, tile=tile, buf=buf):
                    q.wait_ge(s_v.h, s_v.n)
                    s_ld.inc(q.dma_start(out=xt[:, buf, :], in_=x_in[tile:tile + 128, :]), 16)
                p.block.sync(ld_res)
                ldn = s_ld.n

                def projo(t, j=j, ctx_ready=ctx_ready):
                    t.wait_ge(s_s.h, ctx_ready)
                    t.wait_ge(s_v.h, s_v.n)
                    for half in range(2):
                        pp = ps_a if half == 0 else ps_b
                        for kc in range(8):
                            t.matmul(pp[:, :], ctxb[:, kc, j * 128:(j + 1) * 128],
                                     Wo_sb[:, kc, half * 512:(half + 1) * 512],
                                     start=(kc == 0), stop=False)
                        ins = t.matmul(pp[:, :], onesr[0:1, 0:128],
                                       bbf[0:1, 3, half * 512:(half + 1) * 512],
                                       start=False, stop=True)
                        s_t.inc(ins)
                p.block.tensor(projo)
                tno = s_t.n

                def resid(v, j=j, buf=buf, ldn=ldn, tno=tno, stn=s_st.n):
                    v.wait_ge(s_t.h, tno)
                    v.wait_ge(s_ld.h, ldn)
                    v.wait_ge(s_st.h, stn)
                    v.tensor_tensor(out=ysb[:, buf, 0:512], in0=ps_a[:, :],
                                    in1=xt[:, buf, 0:512], op=ALU.add)
                    s_v.inc(v.tensor_tensor(out=ysb[:, buf, 512:1024], in0=ps_b[:, :],
                                            in1=xt[:, buf, 512:1024], op=ALU.add))
                p.block.vector(resid)
                vno = s_v.n

                def st_y(q, tile=tile, buf=buf, vno=vno):
                    q.wait_ge(s_v.h, vno)
                    s_st.inc(q.dma_start(out=x_out[tile:tile + 128, :], in_=ysb[:, buf, :]), 16)
                p.block.sync(st_y)

    def fin(q):
        q.wait_ge(s_st.h, s_st.n)
    p.block.sync(fin)
    p.close()
    return p


def build_ffn_l0(upto="all"):
    p = Prog()
    nc = p.nc
    x_in = nc.dram_tensor("x", [ROWS * T, D], F32, kind="ExternalInput").ap()
    x_out = nc.dram_tensor("y", [ROWS * T, D], F32, kind="ExternalOutput").ap()
    alpha_in = nc.dram_tensor("alpha", [1, 1], F32, kind="ExternalInput").ap()
    wg_in = nc.dram_tensor("w_gate", [D, 8], F32, kind="ExternalInput").ap()
    bg_in = nc.dram_tensor("b_gate", [1, 8], F32, kind="ExternalInput").ap()
    W1m_in = nc.dram_tensor("W1m", [D, DFF], F32, kind="ExternalInput").ap()
    b1m_in = nc.dram_tensor("b1m", [1, DFF], F32, kind="ExternalInput").ap()
    W2m_in = nc.dram_tensor("W2m", [DFF, D], F32, kind="ExternalInput").ap()
    b2m_in = nc.dram_tensor("b2m", [1, D], F32, kind="ExternalInput").ap()
    W1e_in = nc.dram_tensor("W1e", [8 * D, DFF], F32, kind="ExternalInput").ap()
    b1e_in = nc.dram_tensor("b1e", [8, DFF], F32, kind="ExternalInput").ap()
    W2e_in = nc.dram_tensor("W2e", [8 * DFF, D], F32, kind="ExternalInput").ap()
    b2e_in = nc.dram_tensor("b2e", [8, D], F32, kind="ExternalInput").ap()
    consts = declare_consts(p, ["ident", "ones_row", "ones_f32", "ones_col",
                                "iota8", "iota128"])
    xnT_dram = nc.dram_tensor("xnT_scr", [D, ROWS * T], BF16, kind="Internal").ap()
    dbg_idx = nc.dram_tensor("dbg_idx", [128, 32], I32, kind="ExternalOutput").ap()
    dbg_gt = nc.dram_tensor("dbg_gt", [1, 128], F32, kind="ExternalOutput").ap()
    bscr1 = nc.dram_tensor("bscr1", [32, 128], F32, kind="Internal").ap()
    bscr2 = nc.dram_tensor("bscr2", [8, 128], F32, kind="Internal").ap()

    p.start()
    W1 = p.sbuf("W1", [128, 8, DFF], BF16)
    W2 = p.sbuf("W2", [128, 32, D], BF16)
    hbuf = p.sbuf("hbuf", [128, 32, 512], BF16)
    xnT = p.sbuf("xnT", [128, 8, 512], BF16)
    xt = p.sbuf("xt", [128, 2, D], F32)
    yt = xt
    ysb = p.sbuf("ysb", [128, 2, 512], F32)
    xn = p.sbuf("xn", [128, D], BF16)
    xn32 = p.sbuf("xn32", [128, D], F32)
    stat = p.sbuf("stat", [128, 8], F32)
    ident = p.sbuf("ident", [128, 128], BF16)
    onesr = p.sbuf("onesr", [1, 512], BF16)
    onesf = p.sbuf("onesf", [1, 128], F32)
    onesc = p.sbuf("onesc", [128, 1], F32)
    io8 = p.sbuf("io8", [1, 8], F32)
    io128 = p.sbuf("io128", [128, 1], F32)
    b1row = p.sbuf("b1row", [1, DFF], BF16)
    b2row = p.sbuf("b2row", [1, D], BF16)
    wg_sb = p.sbuf("wg_sb", [128, 8, 8], F32)
    gt = p.sbuf("gt", [1, 128], F32)
    gsum = p.sbuf("gsum", [128, 32], F32)
    acol = p.sbuf("acol", [128, 2], F32)
    idxs = p.sbuf("idxs", [128, 32], I32)
    bg32 = p.sbuf("bg32", [32, 128], F32)
    iox = p.sbuf("iox", [128, 1], F32)
    ps_a = p.psum("ps_a", [128, 512], F32)
    ps_b = p.psum("ps_b", [128, 512], F32)
    ps_g = p.psum("ps_g", [128, 512], F32)
    ps_tr = p.psum("ps_tr", [128, 1024], BF16)

    s_ld, s_w, s_t, s_v, s_s, s_st = (p.sems[n] for n in ("ld", "w", "t", "v", "s", "st"))
    s_g = p.sems["g"]
    p.begin()

    def loads(g):
        for kc in range(8):
            s_w.inc(g.dma_start(out=W1[:, kc, :], in_=W1m_in[kc * 128:(kc + 1) * 128, :]), 16)
        for fc in range(32):
            s_w.inc(g.dma_start(out=W2[:, fc, :], in_=W2m_in[fc * 128:(fc + 1) * 128, :]), 16)
        s_w.inc(g.dma_start(out=b1row[:, :], in_=b1m_in[:, :]), 16)
        s_w.inc(g.dma_start(out=b2row[:, :], in_=b2m_in[:, :]), 16)
        for kc in range(8):
            s_w.inc(g.dma_start(out=wg_sb[:, kc, :], in_=wg_in[kc * 128:(kc + 1) * 128, :]), 16)
        s_w.inc(g.dma_start(out=gt[0:1, 32:33], in_=alpha_in[:, :]), 16)
        s_w.inc(g.dma_start(out=gt[0:1, 40:48], in_=bg_in[:, :]), 16)
        s_w.inc(g.dma_start(out=ident[:, :], in_=consts["ident"][:, :]), 16)
        s_w.inc(g.dma_start(out=onesr[:, :], in_=consts["ones_row"][:, :]), 16)
        s_w.inc(g.dma_start(out=onesf[:, :], in_=consts["ones_f32"][:, 0:128]), 16)
        s_w.inc(g.dma_start(out=onesc[:, :], in_=consts["ones_col"][:, :]), 16)
        s_w.inc(g.dma_start(out=io8[:, :], in_=consts["iota8"][:, :]), 16)
        s_w.inc(g.dma_start(out=io128[:, :], in_=consts["iota128"][:, :]), 16)
    p.block.gpsimd(loads)
    w_loaded = s_w.n

    def a_prep_v(v):
        v.wait_ge(s_w.h, w_loaded)
        s_v.inc(v.tensor_scalar(out=gt[0:1, 33:34], in0=gt[0:1, 32:33], scalar1=1.0,
                                scalar2=0.1, op0=ALU.min, op1=ALU.max))
        v.wait_ge(s_v.h, s_v.n)
        s_v.inc(v.tensor_scalar(out=gt[0:1, 34:35], in0=gt[0:1, 33:34], scalar1=-1.0,
                                scalar2=1.0, op0=ALU.mult, op1=ALU.add))
        s_v.inc(v.memset(gsum[:, :], 0.0))
    p.block.vector(a_prep_v)

    def a_bcast_t(t):
        t.wait_ge(s_v.h, s_v.n)
        s_t.inc(t.matmul(ps_g[0:128, 0:1], onesf[0:1, 0:128], gt[0:1, 33:34],
                         start=True, stop=True))
    p.block.tensor(a_bcast_t)

    def a_bcast_v(v):
        v.wait_ge(s_t.h, s_t.n)
        s_v.inc(v.tensor_copy(out=acol[:, 0:1], in_=ps_g[:, 0:1]))
        v.wait_ge(s_v.h, s_v.n)
    p.block.vector(a_bcast_v)

    for r in range(ROWS):
        for c in range(4):
            for j in range(4):
                tile = r * T + c * 512 + j * 128
                buf = j % 2

                def ld_x(q, tile=tile, buf=buf):
                    q.wait_ge(s_v.h, s_v.n)
                    s_ld.inc(q.dma_start(out=xt[:, buf, :], in_=x_in[tile:tile + 128, :]), 16)
                p.block.sync(ld_x)
                ldn = s_ld.n

                def wv0(v, ldn=ldn):
                    v.wait_ge(s_ld.h, ldn)
                p.block.vector(wv0)
                emit_ln(p, xt[:, buf, :], xn, stat, xn)

                def lnf32(v, buf=buf):
                    s_v.inc(v.tensor_scalar(out=xn32[:, :], in0=xt[:, buf, :],
                                            scalar1=stat[:, 2:3], scalar2=stat[:, 4:5],
                                            op0=ALU.subtract, op1=ALU.mult))
                p.block.vector(lnf32)

                def trx(t, j=j):
                    t.wait_ge(s_v.h, s_v.n)
                    for kc in range(8):
                        ins = t.transpose(ps_tr[:, kc * 128:(kc + 1) * 128],
                                          xn[:, kc * 128:(kc + 1) * 128], ident[:, :])
                    s_t.inc(ins)
                    for dc in range(8):
                        ins = t.matmul(ps_g[:, 64 + dc:64 + dc + 1],
                                       xn32[:, dc * 128:(dc + 1) * 128],
                                       onesc[:, 0:1], start=True, stop=True)
                    s_t.inc(ins)
                p.block.tensor(trx)

                def evx(v, j=j, r=r):
                    v.wait_ge(s_t.h, s_t.n)
                    s_v.inc(v.tensor_copy(
                        out=xnT[:, :, j * 128:(j + 1) * 128],
                        in_=ps_tr[:, :].rearrange("p (a b) -> p a b", a=8)))
                    s_v.inc(v.tensor_tensor(out=gsum[:, r * 8:r * 8 + 8],
                                            in0=gsum[:, r * 8:r * 8 + 8],
                                            in1=ps_g[:, 64:72], op=ALU.add))
                p.block.vector(evx)

            vX = s_v.n

            def st_xnT(q, r=r, c=c, vX=vX):
                q.wait_ge(s_v.h, vX)
                for kc in range(8):
                    s_st.inc(q.dma_start(
                        out=xnT_dram[kc * 128:(kc + 1) * 128,
                                     r * T + c * 512:r * T + (c + 1) * 512],
                        in_=xnT[:, kc, :]), 16)
            p.block.sync(st_xnT)

    def glog_t(t, gv=s_v.n):
        t.wait_ge(s_v.h, gv)
        for r in range(ROWS):
            for dc in range(8):
                ins = t.matmul(ps_g[0:1, 96 + r * 8:96 + r * 8 + 8],
                               gsum[:, r * 8 + dc:r * 8 + dc + 1],
                               wg_sb[:, dc, :], start=(dc == 0), stop=(dc == 7))
            s_t.inc(ins)
    p.block.tensor(glog_t)

    def gating_v(v, gl=s_t.n):
        v.wait_ge(s_t.h, gl)
        for r in range(ROWS):
            s_v.inc(v.scalar_tensor_tensor(out=gt[0:1, r * 8:r * 8 + 8],
                                           in0=ps_g[0:1, 96 + r * 8:96 + r * 8 + 8],
                                           scalar=1.0 / T, in1=gt[0:1, 40:48],
                                           op0=ALU.mult, op1=ALU.add))
        v.wait_ge(s_v.h, s_v.n)
        for r in range(ROWS):
            lg = gt[0:1, r * 8:r * 8 + 8]
            s_v.inc(v.tensor_reduce(out=gt[0:1, 48 + r:49 + r], in_=lg, axis=AX.X,
                                    op=ALU.max))
        v.wait_ge(s_v.h, s_v.n)
        for r in range(ROWS):
            lg = gt[0:1, r * 8:r * 8 + 8]
            s_v.inc(v.tensor_scalar(out=gt[0:1, 88:96], in0=lg,
                                    scalar1=gt[0:1, 48 + r:49 + r], scalar2=None,
                                    op0=ALU.is_ge))
            v.wait_ge(s_v.h, s_v.n)
            s_v.inc(v.scalar_tensor_tensor(out=gt[0:1, 104:112], in0=gt[0:1, 88:96],
                                           scalar=1.0, in1=io8[0:1, :],
                                           op0=ALU.bypass, op1=ALU.mult,
                                           accum_out=gt[0:1, 56 + r:57 + r]))
            s_v.inc(v.scalar_tensor_tensor(out=gt[0:1, 96:104], in0=gt[0:1, 88:96],
                                           scalar=NEG, in1=lg,
                                           op0=ALU.mult, op1=ALU.add))
            v.wait_ge(s_v.h, s_v.n)
            s_v.inc(v.tensor_reduce(out=gt[0:1, 52 + r:53 + r], in_=gt[0:1, 96:104],
                                    axis=AX.X, op=ALU.max))
            v.wait_ge(s_v.h, s_v.n)
            s_v.inc(v.tensor_scalar(out=gt[0:1, 88:96], in0=gt[0:1, 96:104],
                                    scalar1=gt[0:1, 52 + r:53 + r], scalar2=None,
                                    op0=ALU.is_ge))
            v.wait_ge(s_v.h, s_v.n)
            s_v.inc(v.scalar_tensor_tensor(out=gt[0:1, 104:112], in0=gt[0:1, 88:96],
                                           scalar=1.0, in1=io8[0:1, :],
                                           op0=ALU.bypass, op1=ALU.mult,
                                           accum_out=gt[0:1, 60 + r:61 + r]))
        v.wait_ge(s_v.h, s_v.n)
        s_v.inc(v.tensor_tensor(out=gt[0:1, 72:76], in0=gt[0:1, 52:56],
                                in1=gt[0:1, 48:52], op=ALU.subtract))
    p.block.vector(gating_v)

    def gating_s(s, gvn=s_v.n):
        s.wait_ge(s_v.h, gvn)
        s_s.inc(s.activation(out=gt[0:1, 76:80], in_=gt[0:1, 72:76], func=AF.Exp))
    p.block.scalar(gating_s)

    def gating_v2(v, gsn=s_s.n):
        v.wait_ge(s_s.h, gsn)
        s_v.inc(v.tensor_scalar(out=gt[0:1, 96:100], in0=gt[0:1, 76:80],
                                scalar1=1.0, scalar2=None, op0=ALU.add))
        v.wait_ge(s_v.h, s_v.n)
        s_v.inc(v.reciprocal(out=gt[0:1, 64:68], in_=gt[0:1, 96:100]))
        v.wait_ge(s_v.h, s_v.n)
        s_v.inc(v.tensor_tensor(out=gt[0:1, 68:72], in0=gt[0:1, 76:80],
                                in1=gt[0:1, 64:68], op=ALU.mult))
        v.wait_ge(s_v.h, s_v.n)
        for r in range(ROWS):
            for k in range(2):
                s_v.inc(v.tensor_scalar(out=gt[0:1, 80 + r * 2 + k:81 + r * 2 + k],
                                        in0=gt[0:1, (64 if k == 0 else 68) + r:
                                               (65 if k == 0 else 69) + r],
                                        scalar1=gt[0:1, 34:35], scalar2=None,
                                        op0=ALU.mult))
    p.block.vector(gating_v2)

    def slot_idx_t(t, gvn=s_v.n):
        t.wait_ge(s_v.h, gvn)
        for s in range(8):
            r, k = divmod(s, 2)
            ecol = (56 if k == 0 else 60) + r
            ins = t.matmul(ps_g[0:128, 128 + s:129 + s], onesf[0:1, 0:128],
                           gt[0:1, ecol:ecol + 1], start=True, stop=True)
        s_t.inc(ins)
    p.block.tensor(slot_idx_t)

    def slot_idx_v(v, tn=s_t.n):
        v.wait_ge(s_t.h, tn)
        s_v.inc(v.tensor_scalar(out=iox[:, 0:1], in0=io128[:, 0:1], scalar1=128.0,
                                scalar2=None, op0=ALU.mult))
        v.wait_ge(s_v.h, s_v.n)
        for s in range(8):
            v.scalar_tensor_tensor(out=idxs[:, 2 * s:2 * s + 1],
                                   in0=ps_g[:, 128 + s:129 + s], scalar=1024.0,
                                   in1=io128[:, 0:1], op0=ALU.mult, op1=ALU.add)
            v.scalar_tensor_tensor(out=idxs[:, 2 * s + 1:2 * s + 2],
                                   in0=ps_g[:, 128 + s:129 + s], scalar=4096.0,
                                   in1=io128[:, 0:1], op0=ALU.mult, op1=ALU.add)
            v.scalar_tensor_tensor(out=idxs[:, 16 + 2 * s:17 + 2 * s],
                                   in0=ps_g[:, 128 + s:129 + s], scalar=4096.0,
                                   in1=iox[:, 0:1], op0=ALU.mult, op1=ALU.add)
            last = v.scalar_tensor_tensor(out=idxs[:, 17 + 2 * s:18 + 2 * s],
                                          in0=ps_g[:, 128 + s:129 + s], scalar=1024.0,
                                          in1=iox[:, 0:1], op0=ALU.mult, op1=ALU.add)
        s_v.inc(last)
    p.block.vector(slot_idx_v)
    idx_ready = s_v.n

    phase1_store_base = s_st.n

    def emit_ffn_pass(which, slot=None):
        import os as _os
        _u = _os.environ.get("FFN_UPTO", "all")
        if which == "main":
            chunks = [(r, c) for r in range(ROWS) for c in range(4)]
            scale_col = acol[:, 0:1]
        else:
            r0, k0 = slot
            chunks = [(r0, c) for c in range(4)]
            if _u in ("slotc1", "slotw1"):
                chunks = chunks[:1]
            scale_col = acol[:, 1:2]

        for (r, c) in chunks:
            ch0 = r * T + c * 512
            ldb = s_t.n

            def ld_xnT(q, ch0=ch0, ldb=ldb):
                q.wait_ge(s_t.h, ldb)
                for kc in range(8):
                    s_ld.inc(q.dma_start(out=xnT[:, kc, :],
                                         in_=xnT_dram[kc * 128:(kc + 1) * 128,
                                                      ch0:ch0 + 512]), 16)
            p.block.sync(ld_xnT)
            ldn = s_ld.n

            sbase = s_s.n

            def w1mm(t, ldn=ldn, sbase=sbase, vb=None):
                t.wait_ge(s_ld.h, ldn)
                t.wait_ge(s_v.h, s_v.n)
                for fc in range(32):
                    if fc >= 2:
                        t.wait_ge(s_s.h, sbase + fc - 1)
                    pp = ps_a if fc % 2 == 0 else ps_b
                    for kc in range(8):
                        t.matmul(pp[:, :], W1[:, kc, fc * 128:(fc + 1) * 128],
                                 xnT[:, kc, :], start=(kc == 0), stop=False)
                    import os as _os2
                    if which == "slot" and _os2.environ.get("FFN_NOB1"):
                        ins = t.matmul(pp[:, :], W1[:, 0, fc * 128:(fc + 1) * 128],
                                       xnT[:, 0, :], start=False, stop=True)
                    else:
                        ins = t.matmul(pp[:, :], b1row[0:1, fc * 128:(fc + 1) * 128],
                                       onesr[0:1, :], start=False, stop=True)
                    s_t.inc(ins)
            p.block.tensor(w1mm)
            t1n = s_t.n
            if which == "slot" and _u == "slotw1":
                def geluz(s, t1n=t1n):
                    s.wait_ge(s_t.h, t1n)
                    s_s.inc(s.copy(out=stat[:, 7:8], in_=stat[:, 6:7]))
                p.block.scalar(geluz)
                continue

            def gelu(s, t1n=t1n):
                for fc in range(32):
                    s.wait_ge(s_t.h, t1n - 32 + fc + 1)
                    pp = ps_a if fc % 2 == 0 else ps_b
                    s_s.inc(s.activation(out=hbuf[:, fc, :], in_=pp[:, :],
                                         func=AF.Gelu_apprx_tanh))
            p.block.scalar(gelu)
            gn = s_s.n

            for j in range(4):
                tile = r * T + c * 512 + j * 128
                buf = j % 2

                def ld_xy(q, tile=tile, buf=buf, which=which, stn=s_st.n):
                    q.wait_ge(s_v.h, s_v.n)
                    if which != "main":
                        q.wait_ge(s_st.h, stn)
                    import os as _os
                    ysrc = x_in if _os.environ.get("FFN_UPTO") == "sloty" else x_out
                    if which == "main":
                        s_ld.inc(q.dma_start(out=xt[:, buf, :],
                                             in_=x_in[tile:tile + 128, :]), 16)
                    else:
                        s_ld.inc(q.dma_start(out=yt[:, buf, :],
                                             in_=ysrc[tile:tile + 128, :]), 16)
                p.block.sync(ld_xy)
                ldn2 = s_ld.n

                def w2mm(t, j=j, gn=gn):
                    t.wait_ge(s_s.h, gn)
                    t.wait_ge(s_v.h, s_v.n)
                    for half in range(2):
                        pp = ps_a if half == 0 else ps_b
                        for fc in range(32):
                            t.matmul(pp[:, :], hbuf[:, fc, j * 128:(j + 1) * 128],
                                     W2[:, fc, half * 512:(half + 1) * 512],
                                     start=(fc == 0), stop=False)
                        ins = t.matmul(pp[:, :], onesr[0:1, 0:128],
                                       b2row[0:1, half * 512:(half + 1) * 512],
                                       start=False, stop=True)
                        s_t.inc(ins)
                p.block.tensor(w2mm)
                t2n = s_t.n

                def resid(v, buf=buf, ldn2=ldn2, t2n=t2n, which=which,
                          scale_col=scale_col, stn=s_st.n):
                    v.wait_ge(s_t.h, t2n)
                    v.wait_ge(s_ld.h, ldn2)
                    v.wait_ge(s_st.h, stn)
                    src = xt if which == "main" else yt
                    s_v.inc(v.scalar_tensor_tensor(out=ysb[:, 0, :], in0=ps_a[:, :],
                                           scalar=scale_col, in1=src[:, buf, 0:512],
                                           op0=ALU.mult, op1=ALU.add))
                    s_v.inc(v.scalar_tensor_tensor(out=ysb[:, 1, :],
                                                   in0=ps_b[:, :], scalar=scale_col,
                                                   in1=src[:, buf, 512:1024],
                                                   op0=ALU.mult, op1=ALU.add))
                p.block.vector(resid)
                vn = s_v.n

                def st_y(q, tile=tile, buf=buf, vn=vn):
                    q.wait_ge(s_v.h, vn - 1)
                    s_st.inc(q.dma_start(out=x_out[tile:tile + 128, 0:512],
                                         in_=ysb[:, 0, :]), 16)
                    q.wait_ge(s_v.h, vn)
                    s_st.inc(q.dma_start(out=x_out[tile:tile + 128, 512:1024],
                                         in_=ysb[:, 1, :]), 16)
                p.block.sync(st_y)

    if upto != "phase0":
        emit_ffn_pass("main")
    main_done_st = s_st.n

    _nslots = {"phase0": 0, "main": 0, "slot1": 1, "slotg": 1, "sloty": 1, "slotc1": 1, "slotw1": 1, "all": 8}[upto]
    for s in range(_nslots):
        r0, k0 = divmod(s, 2)

        def gath(g, s=s, tb=s_t.n, stb=s_st.n, iv=idx_ready):
            g.wait_ge(s_t.h, tb)
            g.wait_ge(s_v.h, iv)
            g.wait_ge(s_st.h, stb)
            import os as _os3
            _wg = _os3.environ.get("FFN_WG", "ind")
            if _wg == "plain":
                for kc in range(8):
                    s_g.inc(g.dma_start(out=W1[:, kc, :],
                                        in_=W1e_in[kc * 128:(kc + 1) * 128, :]), 16)
                for fc in range(32):
                    s_g.inc(g.dma_start(out=W2[:, fc, :],
                                        in_=W2e_in[fc * 128:(fc + 1) * 128, :]), 16)
            elif _wg == "ind":
                for kc in range(8):
                    s_g.inc(g.indirect_dma_start(
                        out=W1[:, kc, :], out_offset=None, in_=W1e_in[:, :],
                        in_offset=IOX(idxs[:, 2 * s:2 * s + 1], 0),
                        element_offset=kc * 128 * DFF), 16)
                for fc in range(32):
                    s_g.inc(g.indirect_dma_start(
                        out=W2[:, fc, :], out_offset=None, in_=W2e_in[:, :],
                        in_offset=IOX(idxs[:, 2 * s + 1:2 * s + 2], 0),
                        element_offset=fc * 128 * D), 16)
            s_g.inc(g.indirect_dma_start(
                out=bg32[0:32, :], out_offset=None,
                in_=b1e_in.rearrange("a b -> (a b) ()"),
                in_offset=IOX(idxs[0:32, 16 + 2 * s:17 + 2 * s], 0)), 16)
            g.wait_ge(s_g.h, s_g.n)
            s_g.inc(g.dma_start(out=bscr1[:, :], in_=bg32[0:32, :]), 16)
            g.wait_ge(s_g.h, s_g.n)
            s_g.inc(g.dma_start(out=b1row[0:1, :],
                                in_=bscr1.rearrange("a b -> () (a b)")), 16)
            s_g.inc(g.indirect_dma_start(
                out=bg32[0:8, :], out_offset=None,
                in_=b2e_in.rearrange("a b -> (a b) ()"),
                in_offset=IOX(idxs[0:8, 17 + 2 * s:18 + 2 * s], 0)), 16)
            g.wait_ge(s_g.h, s_g.n)
            s_g.inc(g.dma_start(out=bscr2[:, :], in_=bg32[0:8, :]), 16)
            g.wait_ge(s_g.h, s_g.n)
            s_g.inc(g.dma_start(out=b2row[0:1, :],
                                in_=bscr2.rearrange("a b -> () (a b)")), 16)
        p.block.gpsimd(gath)
        gw = s_g.n

        def skb_t(t, s=s, gw=gw):
            t.wait_ge(s_g.h, gw)
            s_t.inc(t.matmul(ps_g[0:128, 136 + s:137 + s], onesf[0:1, 0:128],
                             gt[0:1, 80 + s:81 + s], start=True, stop=True))
        p.block.tensor(skb_t)

        def skb_v(v, s=s, tn=s_t.n):
            v.wait_ge(s_t.h, tn)
            s_v.inc(v.tensor_copy(out=acol[:, 1:2], in_=ps_g[:, 136 + s:137 + s]))
            v.wait_ge(s_v.h, s_v.n)
        p.block.vector(skb_v)

        if upto != "slotg":
            emit_ffn_pass("slot", (r0, k0))

    import os as _os4
    def fin(q):
        dbg = _os4.environ.get("FFN_DBG", "")
        if dbg:
            if "w" in dbg:
                q.wait_ge(s_v.h, idx_ready)
            if "1" in dbg:
                s_st.inc(q.dma_start(out=dbg_idx[:, 0:1], in_=idxs[:, 0:1]), 16)
            elif "2" in dbg:
                s_st.inc(q.dma_start(out=dbg_idx[0:1, :], in_=idxs[0:1, :]), 16)
            elif "3" in dbg:
                s_st.inc(q.dma_start(out=dbg_idx[:, 0:16], in_=idxs[:, 0:16]), 16)
            elif "i" in dbg:
                s_st.inc(q.dma_start(out=dbg_idx[:, :], in_=idxs[:, :]), 16)
            else:
                s_st.inc(q.dma_start(out=dbg_idx[0:1, 0:1], in_=idxs[0:1, 0:1]), 16)
            if "g" in dbg:
                s_st.inc(q.dma_start(out=dbg_gt[:, :], in_=gt[:, :]), 16)
            else:
                s_st.inc(q.dma_start(out=dbg_gt[0:1, 0:1], in_=gt[0:1, 0:1]), 16)
        else:
            s_st.inc(q.dma_start(out=dbg_idx[0:1, 0:1], in_=idxs[0:1, 0:1]), 16)
            s_st.inc(q.dma_start(out=dbg_gt[0:1, 0:1], in_=gt[0:1, 0:1]), 16)
        q.wait_ge(s_st.h, s_st.n)
    p.block.sync(fin)
    p.close()
    return p



def build_ffn_l1():
    p = Prog()
    nc = p.nc
    NEXP = 8
    KCAP = 512
    x_in = nc.dram_tensor("x", [ROWS * T, D], F32, kind="ExternalInput").ap()
    x_out = nc.dram_tensor("y", [ROWS * T, D], F32, kind="ExternalOutput").ap()
    gw_in = nc.dram_tensor("gate_w", [NEXP, D], F32, kind="ExternalInput").ap()
    w1_in = nc.dram_tensor("w1", [NEXP * D, DFF], F32, kind="ExternalInput").ap()
    w2_in = nc.dram_tensor("w2", [NEXP * DFF, D], F32, kind="ExternalInput").ap()
    consts = declare_consts(p, ["ident", "ident_f32", "iota128"])
    xn_dram = nc.dram_tensor("xn_scr", [ROWS * T, D], BF16, kind="Internal").ap()
    compact = nc.dram_tensor("compact", [ROWS * NEXP * KCAP + 1, D], BF16, kind="Internal").ap()
    ycomp = nc.dram_tensor("ycomp", [ROWS * NEXP * KCAP + 1, D], BF16, kind="Internal").ap()

    p.start()
    W1 = p.sbuf("W1", [128, 8, DFF], BF16)
    W2 = p.sbuf("W2", [128, 32, D], BF16)
    hbuf = p.sbuf("hbuf", [128, 32, 512], BF16)
    cT = p.sbuf("cT", [128, 8, 512], BF16)
    xt = p.sbuf("xt", [128, 2, D], F32)
    acc = p.sbuf("acc", [128, 2, D], F32)
    gbuf = p.sbuf("gbuf", [128, D], BF16)
    xn = p.sbuf("xn", [128, D], BF16)
    stat = p.sbuf("stat", [128, 8], F32)
    ident = p.sbuf("ident", [128, 128], BF16)
    idf32 = p.sbuf("idf32", [128, 128], F32)
    io128 = p.sbuf("io128", [128, 1], F32)
    gw_sb = p.sbuf("gw_sb", [128, 8, 8], BF16)
    gwr = p.sbuf("gwr", [8, D], BF16)
    probs = p.sbuf("probs", [128, NTILE, 8], F32)
    slotc = p.sbuf("slotc", [128, 16, 32], I32)
    pstage = p.sbuf("pstage", [128, 32], F32)
    bis = p.sbuf("bis", [32, 16], F32)
    ps_a = p.psum("ps_a", [128, 512], F32)
    ps_b = p.psum("ps_b", [128, 512], F32)
    ps_g = p.psum("ps_g", [128, 512], F32)
    ps_tr = p.psum("ps_tr", [128, 1024], BF16)

    vA = hbuf[0:32, 0:16, :].bitcast(F32).rearrange("p a b -> p (a b)")
    vB = cT[0:32, :, :].bitcast(F32).rearrange("p a b -> p (a b)")
    probsT = vA[:, 0:T]
    cum = vA[:, T:2 * T]
    maskv = vB[:, 0:T]
    ones2k = W1[0:32, 0, :].bitcast(F32)
    scr2k = W1[0:32, 1, :].bitcast(F32)
    slotm = W1[0:32, 2, :].bitcast(F32)

    s_ld, s_w, s_t, s_v, s_s, s_st = (p.sems[n] for n in ("ld", "w", "t", "v", "s", "st"))
    s_g = p.sems["g"]
    p.begin()

    def gv(v, inst):
        s_v.inc(inst)
        v.wait_ge(s_v.h, s_v.n)
        return inst

    def loads(g):
        s_w.inc(g.dma_start(out=ident[:, :], in_=consts["ident"][:, :]), 16)
        s_w.inc(g.dma_start(out=idf32[:, :], in_=consts["ident_f32"][:, :]), 16)
        s_w.inc(g.dma_start(out=io128[:, :], in_=consts["iota128"][:, :]), 16)
        s_w.inc(g.dma_start(out=gwr[:, :], in_=gw_in[:, :]), 16)
    p.block.gpsimd(loads)
    w_loaded = s_w.n

    def gwt(t):
        t.wait_ge(s_w.h, w_loaded)
        for dc in range(8):
            ins = t.transpose(ps_tr[:, dc * 8:(dc + 1) * 8],
                              gwr[0:8, dc * 128:(dc + 1) * 128], ident[0:8, 0:8])
        s_t.inc(ins)
    p.block.tensor(gwt)

    def gwe(v, tn=s_t.n):
        v.wait_ge(s_t.h, tn)
        s_v.inc(v.tensor_copy(out=gw_sb[:, :, :],
                              in_=ps_tr[:, 0:64].rearrange("p (a b) -> p a b", a=8)))
    p.block.vector(gwe)

    def zrow_v(v):
        s_v.inc(v.memset(gbuf[:, :], 0.0))
    p.block.vector(zrow_v)

    def zrow_st(q, vn=s_v.n):
        q.wait_ge(s_v.h, vn)
        s_st.inc(q.dma_start(out=ycomp[ROWS * NEXP * KCAP:ROWS * NEXP * KCAP + 1, :],
                             in_=gbuf[0:1, :]), 16)
    p.block.sync(zrow_st)

    for r in range(ROWS):
        for c in range(4):
            for j in range(4):
                gtile = r * 16 + c * 4 + j
                tile = r * T + c * 512 + j * 128
                buf = j % 2

                def ld_x(q, tile=tile, buf=buf):
                    q.wait_ge(s_v.h, s_v.n)
                    s_ld.inc(q.dma_start(out=xt[:, buf, :], in_=x_in[tile:tile + 128, :]), 16)
                p.block.sync(ld_x)
                ldn = s_ld.n

                def wv0(v, ldn=ldn):
                    v.wait_ge(s_ld.h, ldn)
                p.block.vector(wv0)
                emit_ln(p, xt[:, buf, :], xn, stat, xn)
                vx = s_v.n

                def st_xn(q, tile=tile, vx=vx):
                    q.wait_ge(s_v.h, vx)
                    s_st.inc(q.dma_start(out=xn_dram[tile:tile + 128, :], in_=xn[:, :]), 16)
                p.block.sync(st_xn)

                def trx(t, gtile=gtile):
                    t.wait_ge(s_v.h, s_v.n)
                    t.wait_ge(s_s.h, s_s.n)
                    for kc in range(8):
                        ins = t.transpose(ps_tr[:, kc * 128:(kc + 1) * 128],
                                          xn[:, kc * 128:(kc + 1) * 128], ident[:, :])
                    s_t.inc(ins)
                p.block.tensor(trx)

                def evx(v):
                    v.wait_ge(s_t.h, s_t.n)
                    s_v.inc(v.tensor_copy(
                        out=cT[:, :, 0:128],
                        in_=ps_tr[:, :].rearrange("p (a b) -> p a b", a=8)))
                p.block.vector(evx)

                def scmm(t):
                    t.wait_ge(s_v.h, s_v.n)
                    for dc in range(8):
                        ins = t.matmul(ps_g[:, 0:8], cT[:, dc, 0:128], gw_sb[:, dc, :],
                                       start=(dc == 0), stop=(dc == 7))
                    s_t.inc(ins)
                p.block.tensor(scmm)

                def smax_v(v, gtile=gtile, tn=s_t.n):
                    v.wait_ge(s_t.h, tn)
                    gv(v, v.tensor_reduce(out=stat[:, 0:1], in_=ps_g[:, 0:8],
                                          axis=AX.X, op=ALU.max, negate=True))
                p.block.vector(smax_v)

                def smax_s(s, gtile=gtile, vn=s_v.n):
                    s.wait_ge(s_v.h, vn)
                    s_s.inc(s.activation(out=probs[:, gtile, :], in_=ps_g[:, 0:8],
                                         func=AF.Exp, bias=stat[:, 0:1], scale=1.0,
                                         accum_out=stat[:, 1:2]))
                p.block.scalar(smax_s)

                def smax_n(v, gtile=gtile, sn=s_s.n):
                    v.wait_ge(s_s.h, sn)
                    gv(v, v.reciprocal(out=stat[:, 2:3], in_=stat[:, 1:2]))
                    s_v.inc(v.tensor_scalar(out=probs[:, gtile, :], in0=probs[:, gtile, :],
                                            scalar1=stat[:, 2:3], scalar2=None, op0=ALU.mult))
                p.block.vector(smax_n)

    p1done = s_v.n
    for jt in range(16):
        def pst(v, jt=jt, p1done=p1done):
            v.wait_ge(s_v.h, max(p1done, s_v.n))
            v.wait_ge(s_t.h, s_t.n)
            for r in range(ROWS):
                ins = v.tensor_copy(out=pstage[:, r * 8:(r + 1) * 8],
                                    in_=probs[:, r * 16 + jt, :])
            s_v.inc(ins)
        p.block.vector(pst)

        def ptr1(t, vn=s_v.n):
            t.wait_ge(s_v.h, vn)
            s_t.inc(t.transpose(ps_g[0:32, 0:128], pstage[:, :], idf32[:, :]))
        p.block.tensor(ptr1)

        def ptr2(v, jt=jt, tn=s_t.n):
            v.wait_ge(s_t.h, tn)
            s_v.inc(v.tensor_copy(out=probsT[:, jt * 128:(jt + 1) * 128],
                                  in_=ps_g[0:32, 0:128]))
        p.block.vector(ptr2)

    def bisect(v):
        v.wait_ge(s_v.h, s_v.n)
        gv(v, v.memset(bis[:, 0:1], 0.0))
        gv(v, v.memset(bis[:, 1:2], 1.0))
        gv(v, v.memset(ones2k[:, :], 1.0))
        for it in range(34):
            gv(v, v.tensor_tensor(out=bis[:, 2:3], in0=bis[:, 0:1], in1=bis[:, 1:2], op=ALU.add))
            gv(v, v.tensor_scalar(out=bis[:, 2:3], in0=bis[:, 2:3], scalar1=0.5,
                                  scalar2=None, op0=ALU.mult))
            gv(v, v.tensor_scalar(out=scr2k[:, :], in0=probsT[:, :], scalar1=bis[:, 2:3],
                                  scalar2=None, op0=ALU.is_ge, op1=ALU.add,
                                  accum_out=bis[:, 3:4]))
            gv(v, v.tensor_scalar(out=bis[:, 4:5], in0=bis[:, 3:4], scalar1=float(512),
                                  scalar2=None, op0=ALU.is_ge))
            gv(v, v.tensor_tensor(out=bis[:, 5:6], in0=bis[:, 2:3], in1=bis[:, 0:1], op=ALU.subtract))
            gv(v, v.tensor_tensor(out=bis[:, 5:6], in0=bis[:, 5:6], in1=bis[:, 4:5], op=ALU.mult))
            gv(v, v.tensor_tensor(out=bis[:, 0:1], in0=bis[:, 0:1], in1=bis[:, 5:6], op=ALU.add))
            gv(v, v.tensor_scalar(out=bis[:, 6:7], in0=bis[:, 4:5], scalar1=-1.0,
                                  scalar2=1.0, op0=ALU.mult, op1=ALU.add))
            gv(v, v.tensor_tensor(out=bis[:, 7:8], in0=bis[:, 1:2], in1=bis[:, 2:3], op=ALU.subtract))
            gv(v, v.tensor_tensor(out=bis[:, 7:8], in0=bis[:, 7:8], in1=bis[:, 6:7], op=ALU.mult))
            gv(v, v.tensor_tensor(out=bis[:, 1:2], in0=bis[:, 1:2], in1=bis[:, 7:8], op=ALU.subtract))
        gv(v, v.tensor_scalar(out=maskv[:, :], in0=probsT[:, :], scalar1=bis[:, 0:1],
                              scalar2=None, op0=ALU.is_ge))
        gv(v, v.tensor_tensor_scan(out=cum[:, :], data0=ones2k[:, :], data1=maskv[:, :],
                                   initial=0.0, op0=ALU.mult, op1=ALU.add))
        gv(v, v.tensor_scalar(out=bis[:, 8:9], in0=io128[0:32, 0:1], scalar1=float(KCAP),
                              scalar2=None, op0=ALU.mult))
        gv(v, v.tensor_scalar(out=scr2k[:, :], in0=cum[:, :], scalar1=bis[:, 8:9],
                              scalar2=1.0, op0=ALU.add, op1=ALU.subtract))
        gv(v, v.tensor_tensor(out=scr2k[:, :], in0=scr2k[:, :], in1=maskv[:, :], op=ALU.mult))
        gv(v, v.tensor_scalar(out=slotm[:, :], in0=maskv[:, :], scalar1=-16384.0,
                              scalar2=16384.0, op0=ALU.mult, op1=ALU.add))
        gv(v, v.tensor_tensor(out=slotm[:, :], in0=slotm[:, :], in1=scr2k[:, :], op=ALU.add))
    p.block.vector(bisect)

    for jt in range(16):
        def str1(t, jt=jt, vn=s_v.n):
            t.wait_ge(s_v.h, max(vn, s_v.n))
            s_t.inc(t.transpose(ps_g[0:128, 0:32], slotm[:, jt * 128:(jt + 1) * 128],
                                idf32[0:32, 0:32]))
        p.block.tensor(str1)

        def str2(v, jt=jt, tn=s_t.n):
            v.wait_ge(s_t.h, tn)
            s_v.inc(v.tensor_copy(out=slotc[:, jt, :], in_=ps_g[0:128, 0:32]))
        p.block.vector(str2)

    slots_ready = s_v.n

    for r in range(ROWS):
        for jt in range(16):
            gtile = r * 16 + jt
            tile = r * T + jt * 128

            def ld_xn2(q, tile=tile, slots_ready=slots_ready):
                q.wait_ge(s_v.h, s_v.n)
                q.wait_ge(s_st.h, s_st.n)
                s_ld.inc(q.dma_start(out=xn[:, :], in_=xn_dram[tile:tile + 128, :]), 16)
            p.block.sync(ld_xn2)
            ldn = s_ld.n

            def scat(g, r=r, gtile=gtile, ldn=ldn, slots_ready=slots_ready):
                g.wait_ge(s_ld.h, ldn)
                g.wait_ge(s_v.h, slots_ready)
                for e in range(8):
                    s_g.inc(g.indirect_dma_start(
                        out=compact[:, :], out_offset=IOX(slotc[:, jt, r * 8 + e:r * 8 + e + 1], 0),
                        in_=xn[:, :], in_offset=None), 16)
            p.block.gpsimd(scat)
            gn = s_g.n

            def bump(v, gn=gn):
                v.wait_ge(s_g.h, gn)
                s_v.inc(v.tensor_copy(out=stat[:, 3:4], in_=stat[:, 3:4]))
            p.block.vector(bump)

    for e in range(NEXP):
        def ldw(g, e=e, tb=s_t.n):
            g.wait_ge(s_t.h, tb)
            for kc in range(8):
                s_g.inc(g.dma_start(out=W1[:, kc, :],
                                    in_=w1_in[e * D + kc * 128:e * D + (kc + 1) * 128, :]), 16)
            for fc in range(32):
                s_g.inc(g.dma_start(out=W2[:, fc, :],
                                    in_=w2_in[e * DFF + fc * 128:e * DFF + (fc + 1) * 128, :]), 16)
        p.block.gpsimd(ldw)
        wn = s_g.n

        for r in range(ROWS):
            blk = (r * NEXP + e) * KCAP
            for j in range(4):
                def ld_c(q, blk=blk, j=j):
                    q.wait_ge(s_t.h, s_t.n)
                    q.wait_ge(s_g.h, s_g.n)
                    q.wait_ge(s_st.h, s_st.n)
                    s_ld.inc(q.dma_start(out=gbuf[:, :],
                                         in_=compact[blk + j * 128:blk + (j + 1) * 128, :]), 16)
                p.block.sync(ld_c)
                ldn = s_ld.n

                def trc(t, j=j, ldn=ldn):
                    t.wait_ge(s_ld.h, ldn)
                    t.wait_ge(s_v.h, s_v.n)
                    for kc in range(8):
                        ins = t.transpose(ps_tr[:, kc * 128:(kc + 1) * 128],
                                          gbuf[:, kc * 128:(kc + 1) * 128], ident[:, :])
                    s_t.inc(ins)
                p.block.tensor(trc)

                def evc(v, j=j, tn=s_t.n):
                    v.wait_ge(s_t.h, tn)
                    s_v.inc(v.tensor_copy(
                        out=cT[:, :, j * 128:(j + 1) * 128],
                        in_=ps_tr[:, :].rearrange("p (a b) -> p a b", a=8)))
                p.block.vector(evc)

            ctn = s_v.n
            sbase = s_s.n

            def w1mm(t, ctn=ctn, sbase=sbase):
                t.wait_ge(s_v.h, ctn)
                t.wait_ge(s_g.h, s_g.n)
                for fc in range(32):
                    if fc >= 2:
                        t.wait_ge(s_s.h, sbase + fc - 1)
                    pp = ps_a if fc % 2 == 0 else ps_b
                    for kc in range(8):
                        ins = t.matmul(pp[:, :], W1[:, kc, fc * 128:(fc + 1) * 128],
                                       cT[:, kc, :], start=(kc == 0), stop=(kc == 7))
                    s_t.inc(ins)
            p.block.tensor(w1mm)
            t1n = s_t.n

            def gelu(s, t1n=t1n):
                for fc in range(32):
                    s.wait_ge(s_t.h, t1n - 32 + fc + 1)
                    pp = ps_a if fc % 2 == 0 else ps_b
                    s_s.inc(s.activation(out=hbuf[:, fc, :], in_=pp[:, :], func=AF.Gelu))
            p.block.scalar(gelu)
            gn2 = s_s.n

            for j in range(4):
                def w2mm(t, j=j, gn2=gn2):
                    t.wait_ge(s_s.h, gn2)
                    t.wait_ge(s_v.h, s_v.n)
                    for half in range(2):
                        pp = ps_a if half == 0 else ps_b
                        for fc in range(32):
                            ins = t.matmul(pp[:, :], hbuf[:, fc, j * 128:(j + 1) * 128],
                                           W2[:, fc, half * 512:(half + 1) * 512],
                                           start=(fc == 0), stop=(fc == 31))
                        s_t.inc(ins)
                p.block.tensor(w2mm)
                t2n = s_t.n

                def evy(v, j=j, t2n=t2n, stn=s_st.n):
                    v.wait_ge(s_t.h, t2n)
                    v.wait_ge(s_st.h, stn)
                    v.tensor_copy(out=gbuf[:, 0:512], in_=ps_a[:, :])
                    s_v.inc(v.tensor_copy(out=gbuf[:, 512:1024], in_=ps_b[:, :]))
                p.block.vector(evy)
                vn = s_v.n

                def sty(q, blk=blk, j=j, vn=vn):
                    q.wait_ge(s_v.h, vn)
                    s_st.inc(q.dma_start(out=ycomp[blk + j * 128:blk + (j + 1) * 128, :],
                                         in_=gbuf[:, :]), 16)
                p.block.sync(sty)

    for r in range(ROWS):
        for jt in range(16):
            gtile = r * 16 + jt
            tile = r * T + jt * 128
            buf = jt % 2

            def ld_xr(q, tile=tile, buf=buf, stn=s_st.n):
                q.wait_ge(s_v.h, s_v.n)
                q.wait_ge(s_st.h, stn)
                s_ld.inc(q.dma_start(out=xt[:, buf, :], in_=x_in[tile:tile + 128, :]), 16)
            p.block.sync(ld_xr)
            ldn = s_ld.n

            def accv(v, buf=buf, ldn=ldn):
                v.wait_ge(s_ld.h, ldn)
                s_v.inc(v.tensor_copy(out=acc[:, buf, :], in_=xt[:, buf, :]))
            p.block.vector(accv)

            for e in range(8):
                def gby(g, r=r, jt=jt, gtile=gtile, e=e, vn=s_v.n):
                    g.wait_ge(s_v.h, vn)
                    s_g.inc(g.indirect_dma_start(
                        out=gbuf[:, :], out_offset=None,
                        in_=ycomp[:, :],
                        in_offset=IOX(slotc[:, jt, r * 8 + e:r * 8 + e + 1], 0)), 16)
                p.block.gpsimd(gby)
                gn3 = s_g.n

                def accadd(v, gtile=gtile, e=e, buf=buf, gn3=gn3):
                    v.wait_ge(s_g.h, gn3)
                    s_v.inc(v.scalar_tensor_tensor(out=acc[:, buf, :], in0=gbuf[:, :],
                                                   scalar=probs[:, gtile, e:e + 1],
                                                   in1=acc[:, buf, :],
                                                   op0=ALU.mult, op1=ALU.add))
                p.block.vector(accadd)

            vn2 = s_v.n

            def sto(q, tile=tile, buf=buf, vn2=vn2):
                q.wait_ge(s_v.h, vn2)
                s_st.inc(q.dma_start(out=x_out[tile:tile + 128, :], in_=acc[:, buf, :]), 16)
            p.block.sync(sto)

    def fin(q):
        q.wait_ge(s_st.h, s_st.n)
    p.block.sync(fin)
    p.close()
    return p



def build_attn_b1():
    p = Prog()
    nc = p.nc
    EPSP = 1e-8
    TS_ = T // NCORES
    NSLAB = 16
    x_in = nc.dram_tensor("x", [ROWS * T, D], F32, kind="ExternalInput").ap()
    x_out = nc.dram_tensor("y", [ROWS * T, D], F32, kind="ExternalOutput").ap()
    Wd = {n: nc.dram_tensor(n, [D, D], F32, kind="ExternalInput").ap()
          for n in ("wq", "wk", "wv", "wo")}
    w1_in = nc.dram_tensor("w1", [DFF, D], F32, kind="ExternalInput").ap()
    w2_in = nc.dram_tensor("w2", [D, DFF], F32, kind="ExternalInput").ap()
    Bd = {n: nc.dram_tensor(n, [1, D], F32, kind="ExternalInput").ap()
          for n in ("bq", "bk", "bv", "bo", "b2", "g1", "beta1", "g2", "beta2")}
    b1_in = nc.dram_tensor("b1", [1, DFF], F32, kind="ExternalInput").ap()
    tstart_in = nc.dram_tensor("tstart", [1, 1], F32, kind="ExternalInput").ap()
    gbase_in = nc.dram_tensor("gbase", [1, 1], F32, kind="ExternalInput").ap()
    consts = declare_consts(p, ["ident", "ones_row", "iota128", "rowpat"])
    cc_in = nc.dram_tensor("cc_in", [ROWS * T, D], BF16, kind="Internal")
    cc_out = nc.dram_tensor("cc_out", [NCORES * ROWS * T, D], BF16, kind="Internal")
    cc2_in = nc.dram_tensor("cc2_in", [ROWS * T, D], BF16, kind="Internal")
    cc2_out = nc.dram_tensor("cc2_out", [NCORES * ROWS * T, D], BF16, kind="Internal")
    wT = nc.dram_tensor("wT_scr", [128, 98304], BF16, kind="Internal").ap()
    dbg = nc.dram_tensor("dbg", [128, 16384], BF16, kind="ExternalOutput").ap()

    p.start()
    BIG1 = p.sbuf("BIG1", [128, 3, 8, 1024], BF16)
    BIG2 = p.sbuf("BIG2", [128, 16384], BF16)
    BIG3 = p.sbuf("BIG3", [128, 16384], BF16)
    srcs = p.sbuf("srcs", [128, 4, D], BF16)
    x1s = p.sbuf("x1s", [128, 4, D], BF16)
    xTs = p.sbuf("xTs", [128, 8, 512], BF16)
    vtok = p.sbuf("vtok", [128, 4, D], BF16)
    vstage = p.sbuf("vstage", [32, 4, D], BF16)
    cstage = p.sbuf("cstage", [32, 4, D], BF16)
    raw = p.sbuf("raw", [128, 2048], BF16)
    xf = p.sbuf("xf", [128, D], F32)
    x2f = p.sbuf("x2f", [128, D], F32)
    xn = p.sbuf("xn", [128, D], BF16)
    fsb = p.sbuf("fsb", [128, D], BF16)
    stat = p.sbuf("stat", [128, 8], F32)
    nrm = p.sbuf("nrm", [128, 40], F32)
    sig = p.sbuf("sig", [32, 48], F32)
    Pst = p.sbuf("Pst", [96, 16, 32], BF16)
    d2 = p.sbuf("d2", [32, 2, 512], BF16)
    Pm = p.sbuf("Pm", [32, 2, 512], BF16)
    gb = p.sbuf("gb", [128, 5, D], BF16)
    biasc = p.sbuf("biasc", [1, D], BF16)
    b1r = p.sbuf("b1r", [1, 2048], BF16)
    ident = p.sbuf("ident", [128, 128], BF16)
    onesr = p.sbuf("onesr", [1, 512], BF16)
    io128 = p.sbuf("io128", [128, 1], F32)
    rowpat = p.sbuf("rowpat", [128, 1], F32)
    tsb = p.sbuf("tsb", [128, 2], F32)
    idxsl = p.sbuf("idxsl", [128, 64], I32)
    izb = p.sbuf("izb", [128, 64], I32)
    qh = p.sbuf("qh", [128, 16], F32)
    ps_a = p.psum("ps_a", [128, 512], F32)
    ps_b = p.psum("ps_b", [128, 512], F32)
    ps_d = p.psum("ps_d", [32, 512], F32)
    ps_dt = p.psum("ps_dt", [32, 512], F32)
    ps_ctx = p.psum("ps_ctx", [32, 512], F32)
    ps_s = p.psum("ps_s", [128, 512], F32)
    ps_tr = p.psum("ps_tr", [128, 1024], BF16)
    ps_n = p.psum("ps_n", [128, 512], F32)

    s_ld, s_w, s_t, s_v, s_s, s_st = (p.sems[n] for n in ("ld", "w", "t", "v", "s", "st"))
    s_g = p.sems["g"]
    s_cc = p.sems["cc"]
    p.begin()

    w1h = BIG2.rearrange("p (a b) -> p a b", a=8)
    w2h = BIG2.rearrange("p (a b) -> p a b", a=16)
    aos = BIG2.rearrange("p (a b) -> p a b", a=4)
    hfm = BIG3.rearrange("p (a b) -> p a b", a=32)
    qkfm = BIG3.rearrange("p (a b) -> p a b", a=32)

    def loads(g):
        s_w.inc(g.dma_start(out=ident[:, :], in_=consts["ident"][:, :]), 16)
        s_w.inc(g.dma_start(out=onesr[:, :], in_=consts["ones_row"][:, :]), 16)
        s_w.inc(g.dma_start(out=io128[:, :], in_=consts["iota128"][:, :]), 16)
        s_w.inc(g.dma_start(out=rowpat[:, :], in_=consts["rowpat"][:, :]), 16)
        for i2, bn in enumerate(("g1", "beta1", "g2", "beta2", "b2")):
            s_w.inc(g.dma_start(out=BIG1[0:1, 0, i2, :], in_=Bd[bn][:, :]), 16)
        s_w.inc(g.dma_start(out=BIG1[0:1, 1, 0, 0:1], in_=tstart_in[:, :]), 16)
        s_w.inc(g.dma_start(out=BIG1[0:1, 1, 1, 0:1], in_=gbase_in[:, :]), 16)
    p.block.gpsimd(loads)
    w_loaded = s_w.n

    def mkonesf(v):
        v.wait_ge(s_w.h, w_loaded)
        v.memset(raw[:, 1536:1537], 1.0)
        s_v.inc(v.memset(nrm[0:1, 36:37], 1.0))
    p.block.vector(mkonesf)

    cps = []
    for idx in range(10):
        i2b, halfb = idx // 2, idx % 2

        def mm1(t, idx=idx, i2=i2b, half=halfb, vn=s_v.n,
                wn=(cps[idx - 2] if idx >= 2 else None)):
            if wn is None:
                t.wait_ge(s_v.h, vn)
            else:
                t.wait_ge(s_v.h, wn)
            pp = ps_a if half == 0 else ps_b
            s_t.inc(t.matmul(pp[:, :], onesr[0:1, 0:128],
                             BIG1[0:1, 0, i2, half * 512:(half + 1) * 512],
                             start=True, stop=True))
        p.block.tensor(mm1)

        def cp1(v, i2=i2b, half=halfb, tn=s_t.n):
            v.wait_ge(s_t.h, tn)
            pp = ps_a if half == 0 else ps_b
            s_v.inc(v.tensor_copy(out=gb[:, i2, half * 512:(half + 1) * 512],
                                  in_=pp[:, :]))
        p.block.vector(cp1)
        cps.append(s_v.n)

    def bc_t(t, vn=s_v.n):
        t.wait_ge(s_v.h, vn)
        s_t.inc(t.matmul(ps_s[:, 0:2], onesr[0:1, 0:128],
                         BIG1[0:1, 1, 0:2, 0:1].rearrange("p a b -> p (a b)"),
                         start=True, stop=True))
    p.block.tensor(bc_t)
    bcn = s_t.n

    def bc_v(v, bcn=bcn):
        v.wait_ge(s_t.h, bcn)
        v.wait_ge(s_t.h, bcn)
        s_v.inc(v.tensor_copy(out=tsb[:, 0:2], in_=ps_s[:, 0:2]))
        v.wait_ge(s_v.h, s_v.n)
        last = None
        for s2 in range(NSLAB):
            for j2 in range(4):
                last = v.scalar_tensor_tensor(
                    out=idxsl[:, s2 * 4 + j2:s2 * 4 + j2 + 1], in0=tsb[:, 0:1],
                    scalar=1.0, in1=rowpat[:, 0:1], op0=ALU.mult, op1=ALU.add)
        s_v.inc(last)
        v.wait_ge(s_v.h, s_v.n)
        for s2 in range(NSLAB):
            for j2 in range(4):
                last = v.tensor_scalar(
                    out=idxsl[:, s2 * 4 + j2:s2 * 4 + j2 + 1],
                    in0=idxsl[:, s2 * 4 + j2:s2 * 4 + j2 + 1],
                    scalar1=float(s2 * 16 + j2 * 4), scalar2=None, op0=ALU.add)
        s_v.inc(last)
        v.wait_ge(s_v.h, s_v.n)
        for r in range(ROWS):
            for jt in range(16):
                last = v.scalar_tensor_tensor(
                    out=izb[:, r * 16 + jt:r * 16 + jt + 1], in0=tsb[:, 1:2],
                    scalar=float(TS_), in1=io128[:, 0:1], op0=ALU.mult, op1=ALU.add)
        s_v.inc(last)
        v.wait_ge(s_v.h, s_v.n)
        for r in range(ROWS):
            for jt in range(16):
                t0 = jt * 128
                sc = t0 // TS_
                off = sc * (ROWS * T) + r * TS_ + (t0 % TS_)
                last = v.tensor_scalar(
                    out=izb[:, r * 16 + jt:r * 16 + jt + 1],
                    in0=izb[:, r * 16 + jt:r * 16 + jt + 1],
                    scalar1=float(off), scalar2=None, op0=ALU.add)
        s_v.inc(last)
    p.block.vector(bc_v)
    idx_ready = s_v.n

    def gs(v, inst):
        s_v.inc(inst)
        v.wait_ge(s_v.h, s_v.n)

    for i2 in range(NTILE):
        def ld0(g, i2=i2, stn=s_st.n):
            g.wait_ge(s_st.h, stn)
            s_g.inc(g.dma_start(out=fsb[:, :], in_=x_in[i2 * 128:(i2 + 1) * 128, :]), 16)
        p.block.gpsimd(ld0)
        gn0 = s_g.n

        def st0(q, i2=i2, gn0=gn0):
            q.wait_ge(s_g.h, gn0)
            s_st.inc(q.dma_start(out=cc_in.ap()[i2 * 128:(i2 + 1) * 128, :],
                                 in_=fsb[:, :]), 16)
        p.block.sync(st0)

    def ag1(g, stn=s_st.n, iv=idx_ready):
        g.wait_ge(s_st.h, stn)
        g.wait_ge(s_v.h, iv)
        s_cc.inc(g.collective_compute("AllGather", ALU.bypass,
                                      replica_groups=[list(range(NCORES))],
                                      ins=[cc_in.ap().opt()],
                                      outs=[cc_out.ap().opt()]), 1)
    p.block.gpsimd(ag1)
    ag1_done = s_cc.n

    def wtr(src_ap, nchunk, width, dst):
        for j2 in range(nchunk):
            for wh in range(0, width, 2048):
                wid = min(2048, width - wh)

                def ldw(g, j2=j2, wh=wh, wid=wid, src_ap=src_ap, tn=s_t.n):
                    g.wait_ge(s_t.h, tn)
                    s_g.inc(g.dma_start(out=raw[:, 0:wid],
                                        in_=src_ap[j2 * 128:(j2 + 1) * 128,
                                                   wh:wh + wid]), 16)
                p.block.gpsimd(ldw)
                gn = s_g.n
                nk = wid // 128
                for h2 in range((nk + 7) // 8):
                    k0 = h2 * 8
                    k1 = min(k0 + 8, nk)

                    def trw(t, gn=gn, k0=k0, k1=k1):
                        t.wait_ge(s_g.h, gn)
                        t.wait_ge(s_st.h, s_st.n)
                        for kc in range(k0, k1):
                            ins = t.transpose(
                                ps_tr[:, (kc - k0) * 128:(kc - k0 + 1) * 128],
                                raw[:, kc * 128:(kc + 1) * 128], ident[:, :])
                        s_t.inc(ins)
                    p.block.tensor(trw)

                    def evw(v, j2=j2, wh=wh, k0=k0, k1=k1, tn=s_t.n):
                        v.wait_ge(s_t.h, tn)
                        s_v.inc(v.tensor_copy(
                            out=fsb[:, 0:(k1 - k0) * 128].bitcast(BF16) if False else fsb[:, 0:(k1 - k0) * 128],
                            in_=ps_tr[:, 0:(k1 - k0) * 128]))
                    p.block.vector(evw)
                    vn = s_v.n

                    def stw(q, j2=j2, wh=wh, k0=k0, k1=k1, vn=vn, dst=dst):
                        q.wait_ge(s_v.h, vn)
                        for kc in range(k0, k1):
                            s_st.inc(q.dma_start(
                                out=dst(j2, wh // 128 + kc),
                                in_=fsb[:, (kc - k0) * 128:(kc - k0 + 1) * 128]), 16)
                    p.block.sync(stw)

    wtr(w1_in, 32, 1024,
        lambda j2, kc: wT[:, kc * 4096 + j2 * 128: kc * 4096 + (j2 + 1) * 128])
    wtr(w2_in, 8, 4096,
        lambda j2, kc: wT[:, 32768 + kc * 1024 + j2 * 128: 32768 + kc * 1024 + (j2 + 1) * 128])
    for wi, wn2 in enumerate(("wq", "wk", "wv", "wo")):
        wtr(Wd[wn2], 8, 1024,
            lambda j2, kc, wi=wi: wT[:, 65536 + wi * 8192 + kc * 1024 + j2 * 128:
                                     65536 + wi * 8192 + kc * 1024 + (j2 + 1) * 128])
    wprep = s_st.n

    for sl in range(NSLAB):
        def ldqkv(g, tn=s_t.n, wprep=wprep):
            g.wait_ge(s_t.h, tn)
            g.wait_ge(s_st.h, wprep)
            for wi in range(3):
                s_g.inc(g.dma_start(
                    out=BIG1[:, wi, :, :].rearrange("p a b -> p (a b)"),
                    in_=wT[:, 65536 + wi * 8192: 65536 + (wi + 1) * 8192]), 16)
        p.block.gpsimd(ldqkv)
        qkv_w = s_g.n

        for j2 in range(4):
            def lda(g, sl=sl, j2=j2, agd=ag1_done):
                g.wait_ge(s_cc.h, agd)
                g.wait_ge(s_v.h, s_v.n)
                g.wait_ge(s_t.h, s_t.n)
                g.wait_ge(s_st.h, s_st.n)
                s_g.inc(g.indirect_dma_start(
                    out=fsb[:, :], out_offset=None, in_=cc_out.ap()[:, :],
                    in_offset=IOX(idxsl[:, sl * 4 + j2:sl * 4 + j2 + 1], 0)), 16)
            p.block.gpsimd(lda)
            gn = s_g.n

            def cva(v, gn=gn):
                v.wait_ge(s_g.h, gn)
                s_v.inc(v.tensor_copy(out=xf[:, :], in_=fsb[:, :]))
            p.block.vector(cva)
            emit_ln(p, xf, xn, stat, xn)

            def cvb(v, j2=j2):
                s_v.inc(v.tensor_copy(out=srcs[:, j2, :], in_=xn[:, :]))
            p.block.vector(cvb)

            def tra(t, vn=s_v.n):
                t.wait_ge(s_v.h, vn)
                for kc in range(8):
                    ins = t.transpose(ps_tr[:, kc * 128:(kc + 1) * 128],
                                      xn[:, kc * 128:(kc + 1) * 128], ident[:, :])
                s_t.inc(ins)
            p.block.tensor(tra)

            def eva(v, j2=j2, tn=s_t.n):
                v.wait_ge(s_t.h, tn)
                s_v.inc(v.tensor_copy(
                    out=xTs[:, :, j2 * 128:(j2 + 1) * 128],
                    in_=ps_tr[:, :].rearrange("p (a b) -> p a b", a=8)))
            p.block.vector(eva)
        srcT_done = s_v.n

        for wi, dst in ((0, "q"), (1, "k"), (2, "v")):
            def ldbias(g, wi=wi, tn=s_t.n):
                g.wait_ge(s_t.h, tn)
                s_g.inc(g.dma_start(out=biasc[:, :],
                                    in_=Bd[("bq", "bk", "bv")[wi]][:, :]), 16)
            p.block.gpsimd(ldbias)
            bias_w = s_g.n
            for j2 in range(4):
                def mm(t, wi=wi, j2=j2, vn=srcT_done, sn=s_s.n, gw=max(qkv_w, bias_w)):
                    t.wait_ge(s_v.h, max(vn, s_v.n))
                    t.wait_ge(s_s.h, sn)
                    t.wait_ge(s_g.h, gw)
                    for half in range(2):
                        pp = ps_a if half == 0 else ps_b
                        for kc in range(8):
                            t.matmul(pp[:, :], xTs[:, kc, j2 * 128:(j2 + 1) * 128],
                                     BIG1[:, wi, kc, half * 512:(half + 1) * 512],
                                     start=(kc == 0), stop=False)
                        s_t.inc(t.matmul(pp[:, :], onesr[0:1, 0:128],
                                         biasc[0:1, half * 512:(half + 1) * 512],
                                         start=False, stop=True))
                p.block.tensor(mm)
                tn = s_t.n

                def ev(s, tn=tn):
                    s.wait_ge(s_t.h, tn)
                    s.copy(out=raw[:, 0:512], in_=ps_a[:, :])
                    s_s.inc(s.copy(out=raw[:, 512:1024], in_=ps_b[:, :]))
                p.block.scalar(ev)
                sn0 = s_s.n

                def nv(v, dst=dst, sn=sn0):
                    v.wait_ge(s_s.h, sn)
                    gs(v, v.scalar_tensor_tensor(
                        out=xf[:, :], in0=raw[:, 0:1024], scalar=1.0,
                        in1=raw[:, 0:1024], op0=ALU.bypass, op1=ALU.mult))
                    if dst == "v":
                        gs(v, v.tensor_reduce(out=nrm[:, 0:1], in_=xf[:, :],
                                              axis=AX.X, op=ALU.add))
                    else:
                        gs(v, v.tensor_reduce(
                            out=nrm[:, 0:16],
                            in_=xf[:, :].rearrange("p (a b) -> p a b", a=16),
                            axis=AX.X, op=ALU.add))
                p.block.vector(nv)

                def ns(s, dst=dst, vn=s_v.n):
                    s.wait_ge(s_v.h, vn)
                    n2 = 1 if dst == "v" else 16
                    s_s.inc(s.activation(out=nrm[:, 16:16 + n2], in_=nrm[:, 0:n2],
                                         func=AF.Sqrt))
                p.block.scalar(ns)

                def nf(v, wi=wi, j2=j2, dst=dst, sn=s_s.n):
                    v.wait_ge(s_s.h, sn)
                    if dst == "v":
                        gs(v, v.tensor_scalar(out=nrm[:, 32:33], in0=nrm[:, 16:17],
                                              scalar1=EPSP, scalar2=None, op0=ALU.add))
                        gs(v, v.reciprocal(out=nrm[:, 33:34], in_=nrm[:, 32:33]))
                        gs(v, v.tensor_scalar(out=vtok[:, j2, :], in0=raw[:, 0:1024],
                                              scalar1=nrm[:, 33:34], scalar2=None,
                                              op0=ALU.mult))
                    else:
                        gs(v, v.reciprocal(out=qh[:, 0:16], in_=nrm[:, 16:32]))
                        last = None
                        for h in range(16):
                            last = v.tensor_scalar(
                                out=fsb[:, h * 64:(h + 1) * 64],
                                in0=raw[:, h * 64:(h + 1) * 64],
                                scalar1=qh[:, h:h + 1], scalar2=None, op0=ALU.mult)
                        s_v.inc(last)
                p.block.vector(nf)
                vn2 = s_v.n

                if dst == "v":
                    def vmv(g, j2=j2, vn=vn2):
                        g.wait_ge(s_v.h, vn)
                        s_g.inc(g.dma_start(out=vstage[0:32, j2, :],
                                            in_=vtok[96:128, j2, :]), 16)
                    p.block.gpsimd(vmv)
                else:
                    def trq(t, j2=j2, dst=dst, vn=vn2):
                        t.wait_ge(s_v.h, vn)
                        for kc in range(8):
                            ins = t.transpose(ps_tr[:, kc * 128:(kc + 1) * 128],
                                              fsb[:, kc * 128:(kc + 1) * 128],
                                              ident[:, :])
                        s_t.inc(ins)
                    p.block.tensor(trq)

                    def evq(v, j2=j2, dst=dst, tn=s_t.n):
                        v.wait_ge(s_t.h, tn)
                        base = 0 if dst == "q" else 8
                        s_v.inc(v.tensor_copy(
                            out=qkfm[:, base:base + 8, j2 * 128:(j2 + 1) * 128],
                            in_=ps_tr[:, :].rearrange("p (a b) -> p a b", a=8)))
                    p.block.vector(evq)
        if sl == 0:
            def dbg1(q, vn=s_v.n, gn=s_g.n):
                q.wait_ge(s_v.h, vn)
                q.wait_ge(s_g.h, gn)
                s_st.inc(q.dma_start(out=dbg[:, 0:4096],
                                     in_=srcs[:, :, :].rearrange("p a b -> p (a b)")), 16)
                s_st.inc(q.dma_start(out=dbg[:, 8192:10240],
                                     in_=BIG3[:, 0:2048]), 16)
            p.block.sync(dbg1)
        qkv_done = s_v.n
        vhat_done = s_g.n

        for h in range(16):
            hb = (h % 2) * 64
            hc = h // 2

            def dots(t, hb=hb, hc=hc, vn=qkv_done, gdn=vhat_done, sn=s_s.n):
                t.wait_ge(s_v.h, vn)
                t.wait_ge(s_g.h, gdn)
                t.wait_ge(s_s.h, sn)
                for tb in range(16):
                    q_ap = qkfm[hb:hb + 64, hc, tb * 32:(tb + 1) * 32]
                    k_ap = qkfm[hb:hb + 64, 8 + hc, tb * 32:(tb + 1) * 32]
                    t.matmul(ps_d[:, tb * 32:(tb + 1) * 32], q_ap, k_ap,
                             start=True, stop=True)
                    ins = t.matmul(ps_dt[:, tb * 32:(tb + 1) * 32], k_ap, q_ap,
                                   start=True, stop=True)
                s_t.inc(ins)
            p.block.tensor(dots)

            def chain_s(s, tn=s_t.n):
                s.wait_ge(s_t.h, tn)
                s.activation(out=d2[:, 0, :], in_=ps_d[:, :], func=AF.Square)
                s_s.inc(s.activation(out=d2[:, 1, :], in_=ps_dt[:, :], func=AF.Square))
            p.block.scalar(chain_s)

            def chain_v(v, sn=s_s.n):
                v.wait_ge(s_s.h, sn)
                s_v.inc(v.tensor_scalar_min(
                    out=d2[:, :, :].rearrange("p a b -> p (a b)"),
                    in0=d2[:, :, :].rearrange("p a b -> p (a b)"), scalar1=1.0))
            p.block.vector(chain_v)

            def chain_s2(s, vn=s_v.n):
                s.wait_ge(s_v.h, vn)
                s.activation(out=Pm[:, :, :].rearrange("p a b -> p (a b)"),
                             in_=d2[:, :, :].rearrange("p a b -> p (a b)"),
                             func=AF.Sqrt, bias=1.0, scale=-1.0)
                s_s.inc(s.activation(out=Pm[:, :, :].rearrange("p a b -> p (a b)"),
                                     in_=Pm[:, :, :].rearrange("p a b -> p (a b)"),
                                     func=AF.Exp, scale=-1.0))
            p.block.scalar(chain_s2)

            def sigv(v, sn=s_s.n):
                v.wait_ge(s_s.h, sn)
                gs(v, v.tensor_reduce(out=sig[:, 0:16],
                                      in_=Pm[:, 0, :].rearrange("p (a b) -> p a b", a=16),
                                      axis=AX.X, op=ALU.add))
                gs(v, v.reciprocal(out=sig[:, 16:32], in_=sig[:, 0:16]))
                last = None
                for tb in range(16):
                    if tb % 4 in (1, 2):
                        b = (tb % 4) * 32
                        last = v.tensor_copy(out=Pst[b:b + 32, tb, :],
                                             in_=Pm[:, 1, tb * 32:(tb + 1) * 32])
                s_v.inc(last)
            p.block.vector(sigv)
            sgn0 = s_v.n

            for tb in range(16):
                def pv(t, h=h, tb=tb, vn=sgn0):
                    t.wait_ge(s_v.h, max(vn, s_v.n))
                    if tb % 4 == 3:
                        vsrc = vstage[0:32, tb // 4, h * 64:(h + 1) * 64]
                        psrc = Pm[:, 1, tb * 32:(tb + 1) * 32]
                    elif tb % 4 == 0:
                        vsrc = vtok[0:32, tb // 4, h * 64:(h + 1) * 64]
                        psrc = Pm[:, 1, tb * 32:(tb + 1) * 32]
                    else:
                        b = (tb % 4) * 32
                        vsrc = vtok[b:b + 32, tb // 4, h * 64:(h + 1) * 64]
                        psrc = Pst[b:b + 32, tb, :]
                    s_t.inc(t.matmul(ps_ctx[:, (tb % 8) * 64:(tb % 8 + 1) * 64],
                                     psrc, vsrc, start=True, stop=True))
                p.block.tensor(pv)

                def pvev(v, h=h, tb=tb, tn=s_t.n):
                    v.wait_ge(s_t.h, tn)
                    if tb % 4 == 3:
                        dst = cstage[0:32, tb // 4, h * 64:(h + 1) * 64]
                    else:
                        dst = aos[(tb % 4) * 32:(tb % 4) * 32 + 32, tb // 4,
                                  h * 64:(h + 1) * 64]
                    s_v.inc(v.tensor_scalar(
                        out=dst, in0=ps_ctx[:, (tb % 8) * 64:(tb % 8 + 1) * 64],
                        scalar1=sig[:, 16 + tb:17 + tb], scalar2=None, op0=ALU.mult))
                p.block.vector(pvev)

        def s2a(g, vn=s_v.n):
            g.wait_ge(s_v.h, vn)
            for j2 in range(4):
                s_g.inc(g.dma_start(out=aos[96:128, j2, 0:1024],
                                    in_=cstage[0:32, j2, :]), 16)
        p.block.gpsimd(s2a)
        aos_done = s_g.n

        if sl == 0:
            def dbg2(q, gn=s_g.n, vn=s_v.n):
                q.wait_ge(s_g.h, gn)
                q.wait_ge(s_v.h, vn)
                s_st.inc(q.dma_start(out=dbg[0:32, 10240:10752],
                                     in_=Pm[0:32, 0, :]), 16)
            p.block.sync(dbg2)
        def ldwo(g, tn=s_t.n, wprep=wprep):
            g.wait_ge(s_t.h, tn)
            s_g.inc(g.dma_start(
                out=BIG1[:, 0, :, :].rearrange("p a b -> p (a b)"),
                in_=wT[:, 65536 + 3 * 8192: 65536 + 4 * 8192]), 16)
            s_g.inc(g.dma_start(out=biasc[:, :], in_=Bd["bo"][:, :]), 16)
        p.block.gpsimd(ldwo)
        wo_w = s_g.n

        for j2 in range(4):
            def trao(t, j2=j2, gdn=aos_done):
                t.wait_ge(s_g.h, gdn)
                t.wait_ge(s_v.h, s_v.n)
                for kc in range(8):
                    ins = t.transpose(ps_tr[:, kc * 128:(kc + 1) * 128],
                                      aos[:, j2, kc * 128:(kc + 1) * 128], ident[:, :])
                s_t.inc(ins)
            p.block.tensor(trao)

            def evao(v, tn=s_t.n):
                v.wait_ge(s_t.h, tn)
                s_v.inc(v.tensor_copy(out=raw[:, 0:1024], in_=ps_tr[:, 0:1024]))
            p.block.vector(evao)
            aoT = s_v.n
            if sl == 0 and j2 == 0:
                def dbgA(q, vn=s_v.n):
                    q.wait_ge(s_v.h, vn)
                    s_st.inc(q.dma_start(out=dbg[:, 9216:10240],
                                         in_=raw[:, 0:1024]), 16)
                p.block.sync(dbgA)

            def mmo(t, j2=j2, vn=aoT, sn=s_s.n, gw=wo_w):
                t.wait_ge(s_v.h, vn)
                t.wait_ge(s_s.h, sn)
                t.wait_ge(s_g.h, gw)
                for half in range(2):
                    pp = ps_a if half == 0 else ps_b
                    for kc in range(8):
                        t.matmul(pp[:, :], raw[:, kc * 128:(kc + 1) * 128],
                                 BIG1[:, 0, kc, half * 512:(half + 1) * 512],
                                 start=(kc == 0), stop=False)
                    s_t.inc(t.matmul(pp[:, :], onesr[0:1, 0:128],
                                     biasc[0:1, half * 512:(half + 1) * 512],
                                     start=False, stop=True))
            p.block.tensor(mmo)
            tno = s_t.n

            def evo(s, tno=tno):
                s.wait_ge(s_t.h, tno)
                s.copy(out=fsb[:, 0:512], in_=ps_a[:, :])
                s_s.inc(s.copy(out=fsb[:, 512:1024], in_=ps_b[:, :]))
            p.block.scalar(evo)
            sno = s_s.n
            if sl == 0 and j2 == 0:
                def dbgB(g, sn=s_s.n):
                    g.wait_ge(s_s.h, sn)
                    s_g.inc(g.dma_start(out=dbg[:, 7168:8192],
                                        in_=fsb[:, :]), 16)
                p.block.gpsimd(dbgB)

            def nrm_o(v, sno=sno):
                v.wait_ge(s_s.h, sno)
                gs(v, v.scalar_tensor_tensor(out=xf[:, :], in0=fsb[:, :], scalar=1.0,
                                             in1=fsb[:, :], op0=ALU.bypass, op1=ALU.mult))
                gs(v, v.tensor_reduce(out=nrm[:, 0:1], in_=xf[:, :], axis=AX.X,
                                      op=ALU.add))
            p.block.vector(nrm_o)

            def nrms_o(s, vn=s_v.n):
                s.wait_ge(s_v.h, vn)
                s_s.inc(s.activation(out=nrm[:, 1:2], in_=nrm[:, 0:1], func=AF.Sqrt))
            p.block.scalar(nrms_o)

            def fin_o(v, j2=j2, sn=s_s.n):
                v.wait_ge(s_s.h, sn)
                gs(v, v.tensor_scalar(out=nrm[:, 2:3], in0=nrm[:, 1:2],
                                      scalar1=EPSP, scalar2=None, op0=ALU.add))
                gs(v, v.reciprocal(out=nrm[:, 3:4], in_=nrm[:, 2:3]))
                gs(v, v.tensor_scalar(out=fsb[:, :], in0=fsb[:, :],
                                      scalar1=nrm[:, 3:4], scalar2=None, op0=ALU.mult))
                gs(v, v.tensor_tensor(out=xf[:, :], in0=srcs[:, j2, :], in1=fsb[:, :],
                                      op=ALU.add))
            p.block.vector(fin_o)
            if sl == 0 and j2 == 0:
                def dbgC(g, vn=s_v.n):
                    g.wait_ge(s_v.h, vn)
                    s_g.inc(g.dma_start(out=dbg[:, 8192:9216],
                                        in_=xf[:, :]), 16)
                p.block.gpsimd(dbgC)
            emit_uhg_ln(p, xf, x2f, x1s, j2, stat, gb, 0)

            def trx1(t, j2=j2, vn=s_v.n):
                t.wait_ge(s_v.h, vn)
                for kc in range(8):
                    ins = t.transpose(ps_tr[:, kc * 128:(kc + 1) * 128],
                                      x1s[:, j2, kc * 128:(kc + 1) * 128], ident[:, :])
                s_t.inc(ins)
            p.block.tensor(trx1)

            def evx1(v, j2=j2, tn=s_t.n):
                v.wait_ge(s_t.h, tn)
                s_v.inc(v.tensor_copy(
                    out=xTs[:, :, j2 * 128:(j2 + 1) * 128],
                    in_=ps_tr[:, :].rearrange("p (a b) -> p a b", a=8)))
            p.block.vector(evx1)
        if sl == 0:
            def dbg3(q, vn=s_v.n):
                q.wait_ge(s_v.h, vn)
                s_st.inc(q.dma_start(out=dbg[:, 11264:15360],
                                     in_=x1s[:, :, :].rearrange("p a b -> p (a b)")), 16)
                s_st.inc(q.dma_start(out=dbg[:, 15360:16384],
                                     in_=aos[:, 0, 0:1024]), 16)
                for jj in range(1, 4):
                    s_st.inc(q.dma_start(out=dbg[:, 4096 + (jj - 1) * 1024:
                                                 4096 + jj * 1024],
                                         in_=aos[:, jj, 0:1024]), 16)
            p.block.sync(dbg3)
        x1T_done = s_v.n

        sb_h = s_s.n
        for wh in range(2):
            def ldw1s(g, wh=wh, tn=s_t.n, wprep=wprep):
                g.wait_ge(s_t.h, tn)
                g.wait_ge(s_st.h, wprep)
                s_g.inc(g.dma_start(
                    out=BIG2.rearrange("p (a b) -> p a b", a=8),
                    in_=wT[:, 0:32768].rearrange("p (a b) -> p a b", a=8)[
                        :, :, wh * 2048:(wh + 1) * 2048]), 16)
                s_g.inc(g.dma_start(out=b1r[:, :],
                                    in_=b1_in[:, wh * 2048:(wh + 1) * 2048]), 16)
            p.block.gpsimd(ldw1s)
            w1ld = s_g.n

            for fcl in range(16):
                fc = wh * 16 + fcl

                def w1mm(t, fc=fc, fcl=fcl, gdn=w1ld, vn=x1T_done, sb_h=sb_h):
                    t.wait_ge(s_g.h, gdn)
                    t.wait_ge(s_v.h, vn)
                    if fc >= 2:
                        t.wait_ge(s_s.h, sb_h + fc - 1)
                    pp = ps_a if fc % 2 == 0 else ps_b
                    for kc in range(8):
                        t.matmul(pp[:, :], w1h[:, kc, fcl * 128:(fcl + 1) * 128],
                                 xTs[:, kc, :], start=(kc == 0), stop=False)
                    s_t.inc(t.matmul(pp[:, :], b1r[0:1, fcl * 128:(fcl + 1) * 128],
                                     onesr[0:1, :], start=False, stop=True))
                p.block.tensor(w1mm)

                hscr = raw[:, 512:1024] if fc % 2 == 0 else raw[:, 1024:1536]

                def relu_s(s, fc=fc, tn=s_t.n, hscr=hscr):
                    s.wait_ge(s_t.h, tn)
                    pp = ps_a if fc % 2 == 0 else ps_b
                    s.activation(out=hfm[:, fc, :], in_=pp[:, :], func=AF.Relu)
                    s_s.inc(s.activation(out=hscr[:, :], in_=pp[:, :], func=AF.Square))
                p.block.scalar(relu_s)

                def sq_mm(t, fc=fc, sn=s_s.n, hscr=hscr):
                    t.wait_ge(s_s.h, sn)
                    s_t.inc(t.matmul(ps_n[0:1, 0:512], raw[:, 1536:1537], hscr[:, :],
                                     start=(fc == 0), stop=(fc == 31)))
                p.block.tensor(sq_mm)

        hv = s_s.n

        srow = x2f[0:1, 0:512]

        def cpn(v, tn=s_t.n):
            v.wait_ge(s_t.h, tn)
            s_v.inc(v.tensor_copy(out=srow, in_=ps_n[0:1, 0:512]))
        p.block.vector(cpn)

        def trn_mm(t, vn=s_v.n):
            t.wait_ge(s_v.h, vn)
            for j2 in range(4):
                s_t.inc(t.matmul(ps_n[:, j2:j2 + 1],
                                 srow[:, j2 * 128:(j2 + 1) * 128],
                                 nrm[0:1, 36:37], start=True, stop=True))
        p.block.tensor(trn_mm)

        def sqr_s(s, tn=s_t.n):
            s.wait_ge(s_t.h, tn)
            s_s.inc(s.activation(out=nrm[:, 32:36], in_=ps_n[:, 0:4], func=AF.Sqrt))
        p.block.scalar(sqr_s)

        def eps_rec(v, sn=s_s.n):
            v.wait_ge(s_s.h, sn)
            gs(v, v.tensor_scalar(out=nrm[:, 32:36], in0=nrm[:, 32:36],
                                  scalar1=EPSP, scalar2=None, op0=ALU.add))
            gs(v, v.reciprocal(out=nrm[:, 32:36], in_=nrm[:, 32:36]))
        p.block.vector(eps_rec)
        scol_done = s_v.n

        acc0 = vtok.bitcast(F32)
        acc1 = srcs.bitcast(F32)
        for wh in range(2):
            def ldw2s(g, wh=wh, tn=s_t.n, vn=scol_done):
                g.wait_ge(s_t.h, tn)
                g.wait_ge(s_v.h, max(vn, s_v.n))
                s_g.inc(g.dma_start(
                    out=BIG2[:, :],
                    in_=wT[:, 32768 + wh * 16384:32768 + (wh + 1) * 16384]), 16)
            p.block.gpsimd(ldw2s)
            w2ld = s_g.n

            for j2 in range(4):
                def w2mm(t, j2=j2, wh=wh, gdn=w2ld, vn=s_v.n):
                    t.wait_ge(s_g.h, gdn)
                    t.wait_ge(s_v.h, vn)
                    for half in range(2):
                        pp = ps_a if half == 0 else ps_b
                        for fcl in range(16):
                            fc = wh * 16 + fcl
                            ins = t.matmul(pp[:, :], hfm[:, fc, j2 * 128:(j2 + 1) * 128],
                                           w2h[:, fcl, half * 512:(half + 1) * 512],
                                           start=(fcl == 0), stop=(fcl == 15))
                        s_t.inc(ins)
                p.block.tensor(w2mm)
                t2n = s_t.n

                if wh == 0:
                    def acc2(v, j2=j2, t2n=t2n):
                        v.wait_ge(s_t.h, t2n)
                        v.tensor_copy(out=acc0[:, j2, :], in_=ps_a[:, :])
                        s_v.inc(v.tensor_copy(out=acc1[:, j2, :], in_=ps_b[:, :]))
                    p.block.vector(acc2)
                else:
                    def ffep(v, j2=j2, t2n=t2n):
                        v.wait_ge(s_t.h, t2n)
                        gs(v, v.scalar_tensor_tensor(out=xf[:, 0:512], in0=ps_a[:, :],
                                                     scalar=1.0, in1=acc0[:, j2, :],
                                                     op0=ALU.bypass, op1=ALU.add))
                        gs(v, v.scalar_tensor_tensor(out=xf[:, 512:1024], in0=ps_b[:, :],
                                                     scalar=1.0, in1=acc1[:, j2, :],
                                                     op0=ALU.bypass, op1=ALU.add))
                        gs(v, v.tensor_scalar(out=xf[:, :], in0=xf[:, :],
                                              scalar1=nrm[:, 32 + j2:33 + j2],
                                              scalar2=None, op0=ALU.mult))
                        gs(v, v.tensor_tensor(out=xf[:, :], in0=xf[:, :],
                                              in1=gb[:, 4, :], op=ALU.add))
                        gs(v, v.scalar_tensor_tensor(out=x2f[:, :], in0=xf[:, :],
                                                     scalar=1.0, in1=xf[:, :],
                                                     op0=ALU.bypass, op1=ALU.mult))
                        gs(v, v.tensor_reduce(out=nrm[:, 25:26], in_=x2f[:, :],
                                              axis=AX.X, op=ALU.add))
                    p.block.vector(ffep)

                    def ffs(s, vn=s_v.n):
                        s.wait_ge(s_v.h, vn)
                        s_s.inc(s.activation(out=nrm[:, 26:27], in_=nrm[:, 25:26],
                                             func=AF.Sqrt))
                    p.block.scalar(ffs)

                    def fff(v, j2=j2, sn=s_s.n):
                        v.wait_ge(s_s.h, sn)
                        gs(v, v.tensor_scalar(out=nrm[:, 27:28], in0=nrm[:, 26:27],
                                              scalar1=EPSP, scalar2=None, op0=ALU.add))
                        gs(v, v.reciprocal(out=nrm[:, 28:29], in_=nrm[:, 27:28]))
                        gs(v, v.tensor_scalar(out=xf[:, :], in0=xf[:, :],
                                              scalar1=nrm[:, 28:29], scalar2=None,
                                              op0=ALU.mult))
                        gs(v, v.tensor_tensor(out=xf[:, :], in0=x1s[:, j2, :],
                                              in1=xf[:, :], op=ALU.add))
                    p.block.vector(fff)
                    emit_uhg_ln(p, xf, x2f, None, j2, stat, gb, 2, out_f32=x2f)

                    def finn(v):
                        gs(v, v.scalar_tensor_tensor(out=xf[:, :], in0=x2f[:, :],
                                                     scalar=1.0, in1=x2f[:, :],
                                                     op0=ALU.bypass, op1=ALU.mult))
                        gs(v, v.tensor_reduce(out=nrm[:, 29:30], in_=xf[:, :],
                                              axis=AX.X, op=ALU.add))
                    p.block.vector(finn)

                    def fins(s, vn=s_v.n):
                        s.wait_ge(s_v.h, vn)
                        s_s.inc(s.activation(out=nrm[:, 30:31], in_=nrm[:, 29:30],
                                             func=AF.Sqrt))
                    p.block.scalar(fins)

                    def finv(v, sn=s_s.n, stn=s_st.n):
                        v.wait_ge(s_s.h, sn)
                        v.wait_ge(s_st.h, stn)
                        gs(v, v.reciprocal(out=nrm[:, 31:32], in_=nrm[:, 30:31]))
                        gs(v, v.tensor_scalar(out=fsb[:, :], in0=x2f[:, :],
                                              scalar1=nrm[:, 31:32], scalar2=None,
                                              op0=ALU.mult))
                    p.block.vector(finv)
                    fdone = s_v.n

                    def stf(q, sl=sl, j2=j2, fdone=fdone):
                        q.wait_ge(s_v.h, fdone)
                        t0 = sl * 16 + j2 * 4
                        dest = cc2_in.ap().rearrange("(r t) d -> t r d", t=TS_)[
                            t0:t0 + 4, :, :]
                        s_st.inc(q.dma_start(out=dest, in_=fsb[:, :]), 16)
                    p.block.sync(stf)

    def ag2(g, stn=s_st.n):
        g.wait_ge(s_st.h, stn)
        s_cc.inc(g.collective_compute("AllGather", ALU.bypass,
                                      replica_groups=[list(range(NCORES))],
                                      ins=[cc2_in.ap().opt()],
                                      outs=[cc2_out.ap().opt()]), 1)
    p.block.gpsimd(ag2)
    ag2_done = s_cc.n

    for r in range(ROWS):
        for jt in range(16):
            tile = r * T + jt * 128

            def ldz(q, tile=tile):
                q.wait_ge(s_v.h, s_v.n)
                s_ld.inc(q.dma_start(out=x2f[:, :], in_=x_in[tile:tile + 128, :]), 16)
            p.block.sync(ldz)
            ldn = s_ld.n

            def gfz(g, r=r, jt=jt, agd=ag2_done, vn=s_v.n):
                g.wait_ge(s_cc.h, agd)
                g.wait_ge(s_v.h, vn)
                s_g.inc(g.indirect_dma_start(
                    out=fsb[:, :], out_offset=None, in_=cc2_out.ap()[:, :],
                    in_offset=IOX(izb[:, r * 16 + jt:r * 16 + jt + 1], 0)), 16)
            p.block.gpsimd(gfz)
            gn = s_g.n

            def addz(v, ldn=ldn, gn=gn, stn=s_st.n):
                v.wait_ge(s_ld.h, ldn)
                v.wait_ge(s_g.h, gn)
                v.wait_ge(s_st.h, stn)
                s_v.inc(v.tensor_tensor(out=xf[:, :], in0=x2f[:, :], in1=fsb[:, :],
                                        op=ALU.add))
            p.block.vector(addz)
            vn3 = s_v.n

            def stz(q, tile=tile, vn3=vn3):
                q.wait_ge(s_v.h, vn3)
                s_st.inc(q.dma_start(out=x_out[tile:tile + 128, :], in_=xf[:, :]), 16)
            p.block.sync(stz)

    def fin(q):
        q.wait_ge(s_st.h, s_st.n)
    p.block.sync(fin)
    p.close()
    return p



def build_attn_b0():
    p = Prog()
    nc = p.nc
    KVL = 256
    alpha = 1.0 / math.sqrt(HD)
    x_in = nc.dram_tensor("x", [ROWS * T, D], F32, kind="ExternalInput").ap()
    x_out = nc.dram_tensor("y", [ROWS * T, D], F32, kind="ExternalOutput").ap()
    Wdq_in = nc.dram_tensor("W_dq", [KVL, D], F32, kind="ExternalInput").ap()
    Wuq_in = nc.dram_tensor("W_uq", [D, KVL], F32, kind="ExternalInput").ap()
    Wdkv_in = nc.dram_tensor("W_dkv", [KVL, D], F32, kind="ExternalInput").ap()
    Wuk_in = nc.dram_tensor("W_uk", [D, KVL], F32, kind="ExternalInput").ap()
    Wuv_in = nc.dram_tensor("W_uv", [D, KVL], F32, kind="ExternalInput").ap()
    Wo_in = nc.dram_tensor("W_o", [D, D], F32, kind="ExternalInput").ap()
    consts = declare_consts(p, ["ident", "ones_row", "iota128"])

    p.start()
    Mq = p.sbuf("Mq", [128, 8, D], BF16)
    MqT = p.sbuf("MqT", [128, 8, D], BF16)
    Mk = p.sbuf("Mk", [128, 8, KVL], BF16)
    Mv = p.sbuf("Mv", [128, 2, D], BF16)
    Wdq = p.sbuf("Wdq", [128, 2, D], BF16)
    WuqT = p.sbuf("WuqT", [128, 2, D], BF16)
    Wuk = p.sbuf("Wuk", [128, 8, KVL], BF16)
    Wuv = p.sbuf("Wuv", [128, 8, KVL], BF16)
    Wdkv = p.sbuf("Wdkv", [128, 2, D], BF16)
    WdkvT = p.sbuf("WdkvT", [128, 8, KVL], BF16)
    Wo_sb = p.sbuf("Wo_sb", [128, 8, D], BF16)
    woT = p.sbuf("woT", [128, D], BF16)
    xnT = p.sbuf("xnT", [128, 8, T], BF16)
    ckvT = p.sbuf("ckvT", [128, 2, T], BF16)
    ckvt = p.sbuf("ckvt", [128, 16, KVL], BF16)
    qfm = p.sbuf("qfm", [128, 8, 128], BF16)
    qlb = p.sbuf("qlb", [128, 256], BF16)
    ctxb = p.sbuf("ctxb", [128, 256], BF16)
    Pb = p.sbuf("Pb", [128, T], BF16)
    pT = p.sbuf("pT", [128, 2, 128], BF16)
    msk = p.sbuf("msk", [128, 128], F32)
    rowm = p.sbuf("rowm", [128, 128], BF16)
    xt = p.sbuf("xt", [128, 2, D], F32)
    ysb = p.sbuf("ysb", [128, D], F32)
    xn = p.sbuf("xn", [128, D], BF16)
    stat = p.sbuf("stat", [128, 8], F32)
    sgn = p.sbuf("sgn", [128, 8], F32)
    ident = p.sbuf("ident", [128, 128], BF16)
    onesr = p.sbuf("onesr", [1, 512], BF16)
    io128 = p.sbuf("io128", [128, 1], F32)
    ps_big = p.psum("ps_big", [128, 2048], F32)
    ps_y = p.psum("ps_y", [128, 1024], F32)
    ps_lc = p.psum("ps_lc", [128, 512], F32)
    ps_tp = p.psum("ps_tp", [128, 1024], BF16)

    s_ld, s_w, s_t, s_v, s_s, s_st = (p.sems[n] for n in ("ld", "w", "t", "v", "s", "st"))
    p.begin()

    def gs(v, inst):
        s_v.inc(inst)
        v.wait_ge(s_v.h, s_v.n)

    def loads(g):
        s_w.inc(g.dma_start(out=ident[:, :], in_=consts["ident"][:, :]), 16)
        s_w.inc(g.dma_start(out=onesr[:, :], in_=consts["ones_row"][:, :]), 16)
        s_w.inc(g.dma_start(out=io128[:, :], in_=consts["iota128"][:, :]), 16)
        for lc in range(2):
            s_w.inc(g.dma_start(out=Wdq[:, lc, :],
                                in_=Wdq_in[lc * 128:(lc + 1) * 128, :]), 16)
            s_w.inc(g.dma_start(out=Wdkv[:, lc, :],
                                in_=Wdkv_in[lc * 128:(lc + 1) * 128, :]), 16)
        for oc in range(8):
            s_w.inc(g.dma_start(out=Wuk[:, oc, :],
                                in_=Wuk_in[oc * 128:(oc + 1) * 128, :]), 16)
            s_w.inc(g.dma_start(out=Wuv[:, oc, :],
                                in_=Wuv_in[oc * 128:(oc + 1) * 128, :]), 16)
            s_w.inc(g.dma_start(out=Wo_sb[:, oc, :],
                                in_=Wo_in[oc * 128:(oc + 1) * 128, :]), 16)
        for oc in range(8):
            s_w.inc(g.dma_start(out=Mk[:, oc, :],
                                in_=Wuq_in[oc * 128:(oc + 1) * 128, :]), 16)
    p.block.gpsimd(loads)
    w_loaded = s_w.n

    for lc in range(2):
        def truq(t, lc=lc, wn=w_loaded, vn=s_v.n):
            t.wait_ge(s_w.h, wn)
            t.wait_ge(s_v.h, vn)
            for oc in range(8):
                ins = t.transpose(ps_tp[:, oc * 128:(oc + 1) * 128],
                                  Mk[:, oc, lc * 128:(lc + 1) * 128], ident[:, :])
            s_t.inc(ins)
        p.block.tensor(truq)

        def evuq(v, lc=lc, tn=s_t.n):
            v.wait_ge(s_t.h, tn)
            s_v.inc(v.tensor_copy(out=WuqT[:, lc, :], in_=ps_tp[:, :]))
        p.block.vector(evuq)

    for lc in range(2):
        def trdkv(t, lc=lc, vn=s_v.n):
            t.wait_ge(s_v.h, vn)
            for dc in range(8):
                ins = t.transpose(ps_tp[:, dc * 128:(dc + 1) * 128],
                                  Wdkv[:, lc, dc * 128:(dc + 1) * 128], ident[:, :])
            s_t.inc(ins)
        p.block.tensor(trdkv)

        def evdkv(v, lc=lc, tn=s_t.n):
            v.wait_ge(s_t.h, tn)
            s_v.inc(v.tensor_copy(
                out=WdkvT[:, :, lc * 128:(lc + 1) * 128],
                in_=ps_tp[:, :].rearrange("p (a b) -> p a b", a=8)))
        p.block.vector(evdkv)
    uqT_done = s_v.n

    for dc in range(8):
        def mmq(t, dc=dc, vn=uqT_done, sn=s_s.n):
            t.wait_ge(s_v.h, vn)
            t.wait_ge(s_s.h, sn)
            for oh in range(2):
                for lc in range(2):
                    ins = t.matmul(ps_y[:, oh * 512:(oh + 1) * 512],
                                   Wdq[:, lc, dc * 128:(dc + 1) * 128],
                                   WuqT[:, lc, oh * 512:(oh + 1) * 512],
                                   start=(lc == 0), stop=(lc == 1))
            s_t.inc(ins)
        p.block.tensor(mmq)

        def evq(s, dc=dc, tn=s_t.n):
            s.wait_ge(s_t.h, tn)
            s_s.inc(s.copy(out=Mq[:, dc, :], in_=ps_y[:, :]))
        p.block.scalar(evq)
    for oc in range(8):
        def mmqt(t, oc=oc, sn=s_s.n):
            t.wait_ge(s_s.h, sn)
            for dh in range(2):
                for lc in range(2):
                    ins = t.matmul(ps_y[:, dh * 512:(dh + 1) * 512],
                                   WuqT[:, lc, oc * 128:(oc + 1) * 128],
                                   Wdq[:, lc, dh * 512:(dh + 1) * 512],
                                   start=(lc == 0), stop=(lc == 1))
            s_t.inc(ins)
        p.block.tensor(mmqt)

        def evqt(s, oc=oc, tn=s_t.n):
            s.wait_ge(s_t.h, tn)
            s_s.inc(s.copy(out=MqT[:, oc, :], in_=ps_y[:, :]))
        p.block.scalar(evqt)
    mq_done = s_s.n

    for dc in range(8):
        def mmk(t, dc=dc, sn=mq_done, vn=s_v.n):
            t.wait_ge(s_s.h, sn)
            t.wait_ge(s_v.h, vn)
            for oc in range(8):
                ins = t.matmul(ps_lc[:, 0:256], MqT[:, oc, dc * 128:(dc + 1) * 128],
                               Wuk[:, oc, :], start=(oc == 0), stop=(oc == 7))
            s_t.inc(ins)
        p.block.tensor(mmk)

        def evk(v, dc=dc, tn=s_t.n):
            v.wait_ge(s_t.h, tn)
            s_v.inc(v.tensor_copy(out=Mk[:, dc, :], in_=ps_lc[:, 0:256]))
        p.block.vector(evk)

    for cc in range(8):
        def trwo(t, cc=cc, vn=s_v.n):
            t.wait_ge(s_v.h, vn)
            for rc in range(8):
                ins = t.transpose(ps_tp[:, rc * 128:(rc + 1) * 128],
                                  Wo_sb[:, rc, cc * 128:(cc + 1) * 128], ident[:, :])
            s_t.inc(ins)
        p.block.tensor(trwo)

        def evwo(v, cc=cc, tn=s_t.n):
            v.wait_ge(s_t.h, tn)
            s_v.inc(v.tensor_copy(out=woT[:, :], in_=ps_tp[:, :]))
        p.block.vector(evwo)

        def mmv(t, cc=cc, vn=s_v.n):
            t.wait_ge(s_v.h, vn)
            for oh in range(2):
                ppc = ps_y if True else None
                ins = t.matmul(ps_y[:, oh * 512:(oh + 1) * 512],
                               Wuv[:, cc, 0:128],
                               woT[:, oh * 512:(oh + 1) * 512],
                               start=(cc == 0), stop=(cc == 7))
                ins = t.matmul(ps_big[:, oh * 512:(oh + 1) * 512],
                               Wuv[:, cc, 128:256],
                               woT[:, oh * 512:(oh + 1) * 512],
                               start=(cc == 0), stop=(cc == 7))
            s_t.inc(ins)
        p.block.tensor(mmv)
    mv_t = s_t.n

    def evmv(v, tn=mv_t):
        v.wait_ge(s_t.h, tn)
        v.tensor_copy(out=Mv[:, 0, :], in_=ps_y[:, :])
        s_v.inc(v.tensor_copy(out=Mv[:, 1, :], in_=ps_big[:, 0:1024]))
    p.block.vector(evmv)

    def mkrow(v, wn=w_loaded):
        v.wait_ge(s_w.h, wn)
        gs(v, v.memset(rowm[:, :], 1.0))
        gs(v, v.tensor_scalar(out=rowm[:, :], in0=rowm[:, :],
                              scalar1=io128[:, 0:1], scalar2=None, op0=ALU.mult))
    p.block.vector(mkrow)

    def trrow(t, vn=s_v.n):
        t.wait_ge(s_v.h, vn)
        s_t.inc(t.transpose(ps_tp[:, 0:128], rowm[:, :], ident[:, :]))
    p.block.tensor(trrow)

    def mkmsk(v, tn=s_t.n):
        v.wait_ge(s_t.h, tn)
        gs(v, v.tensor_copy(out=msk[:, :], in_=ps_tp[:, 0:128]))
        gs(v, v.tensor_scalar(out=msk[:, :], in0=msk[:, :],
                              scalar1=io128[:, 0:1], scalar2=None, op0=ALU.subtract))
        gs(v, v.tensor_scalar(out=msk[:, :], in0=msk[:, :],
                              scalar1=0.0, scalar2=-1e28, op0=ALU.max, op1=ALU.mult))
    p.block.vector(mkmsk)
    prep_done = s_v.n

    for r in range(ROWS):
        base = r * T
        for c16 in range(16):
            buf = c16 % 2

            def ld_x(q, c16=c16, buf=buf, vn=s_v.n):
                q.wait_ge(s_v.h, vn)
                s_ld.inc(q.dma_start(out=xt[:, buf, :],
                                     in_=x_in[base + c16 * 128:base + (c16 + 1) * 128, :]), 16)
            p.block.sync(ld_x)
            ldn = s_ld.n

            def wv0(v, ldn=ldn, pn=prep_done):
                v.wait_ge(s_ld.h, ldn)
                v.wait_ge(s_v.h, pn)
            p.block.vector(wv0)
            emit_ln(p, xt[:, buf, :], xn, stat, xn)

            def trx(t, vn=s_v.n, sn=s_v.n):
                t.wait_ge(s_v.h, vn)
                for kc in range(8):
                    ins = t.transpose(ps_tp[:, kc * 128:(kc + 1) * 128],
                                      xn[:, kc * 128:(kc + 1) * 128], ident[:, :])
                s_t.inc(ins)
            p.block.tensor(trx)

            def evx(v, c16=c16, tn=s_t.n):
                v.wait_ge(s_t.h, tn)
                s_v.inc(v.tensor_copy(
                    out=xnT[:, :, c16 * 128:(c16 + 1) * 128],
                    in_=ps_tp[:, :].rearrange("p (a b) -> p a b", a=8)))
            p.block.vector(evx)
        xnT_ready = s_v.n

        for tb in range(4):
            def mmckv(t, tb=tb, vn=xnT_ready, sn=s_s.n):
                t.wait_ge(s_v.h, vn)
                t.wait_ge(s_s.h, sn)
                for lc in range(2):
                    for dc in range(8):
                        ins = t.matmul(ps_big[:, lc * 512:(lc + 1) * 512],
                                       WdkvT[:, dc, lc * 128:(lc + 1) * 128],
                                       xnT[:, dc, tb * 512:(tb + 1) * 512],
                                       start=(dc == 0), stop=(dc == 7))
                s_t.inc(ins)
            p.block.tensor(mmckv)

            def evckv(s, tb=tb, tn=s_t.n):
                s.wait_ge(s_t.h, tn)
                s.copy(out=ckvT[:, 0, tb * 512:(tb + 1) * 512], in_=ps_big[:, 0:512])
                s_s.inc(s.copy(out=ckvT[:, 1, tb * 512:(tb + 1) * 512],
                               in_=ps_big[:, 512:1024]))
            p.block.scalar(evckv)
        for s4 in range(4):
            def trck(t, s4=s4, sn=s_s.n, vn=s_v.n):
                t.wait_ge(s_s.h, sn)
                t.wait_ge(s_v.h, vn)
                for si in range(4):
                    sc = s4 * 4 + si
                    for lc in range(2):
                        ins = t.transpose(ps_tp[:, si * 256 + lc * 128:
                                                si * 256 + (lc + 1) * 128],
                                          ckvT[:, lc, sc * 128:(sc + 1) * 128],
                                          ident[:, :])
                s_t.inc(ins)
            p.block.tensor(trck)

            def evck(v, s4=s4, tn=s_t.n):
                v.wait_ge(s_t.h, tn)
                s_v.inc(v.tensor_copy(
                    out=ckvt[:, s4 * 4:(s4 + 1) * 4, :],
                    in_=ps_tp[:, :].rearrange("p (a b) -> p a b", a=4)))
            p.block.vector(evck)
        ckv_ready = s_v.n

        for i in range(16):
            nsb = i + 1

            def mmqf(t, i=i, vn=max(ckv_ready, s_v.n)):
                t.wait_ge(s_v.h, vn)
                for oc in range(8):
                    for dc in range(8):
                        ins = t.matmul(ps_y[:, oc * 128:(oc + 1) * 128],
                                       Mq[:, dc, oc * 128:(oc + 1) * 128],
                                       xnT[:, dc, i * 128:(i + 1) * 128],
                                       start=(dc == 0), stop=(dc == 7))
                s_t.inc(ins)
            p.block.tensor(mmqf)

            def evqf(v, tn=s_t.n):
                v.wait_ge(s_t.h, tn)
                s_v.inc(v.tensor_copy(
                    out=qfm[:, :, :],
                    in_=ps_y[:, :].rearrange("p (a b) -> p a b", a=8)))
            p.block.vector(evqf)
            qf_ready = s_v.n

            for h in range(H):
                hb = (h % 2) * 64
                hc = h // 2

                def mmql(t, hb=hb, hc=hc, vn=qf_ready, tn=s_t.n):
                    t.wait_ge(s_v.h, vn)
                    for lc in range(2):
                        ins = t.matmul(ps_lc[:, lc * 128:(lc + 1) * 128],
                                       Mk[hb:hb + 64, hc, lc * 128:(lc + 1) * 128],
                                       qfm[hb:hb + 64, hc, :],
                                       start=True, stop=True)
                    s_t.inc(ins)
                p.block.tensor(mmql)

                def evql(s, tn=s_t.n):
                    s.wait_ge(s_t.h, tn)
                    s_s.inc(s.copy(out=qlb[:, :], in_=ps_lc[:, 0:256]))
                p.block.scalar(evql)

                def scores(t, i=i, sn=s_s.n):
                    t.wait_ge(s_s.h, sn)
                    for sb0 in range(0, (i + 1) * 128, 512):
                        ww = min(512, (i + 1) * 128 - sb0)
                        for lc in range(2):
                            ins = t.matmul(ps_big[:, sb0:sb0 + ww],
                                           qlb[:, lc * 128:(lc + 1) * 128],
                                           ckvT[:, lc, sb0:sb0 + ww],
                                           start=(lc == 0), stop=(lc == 1))
                    s_t.inc(ins)
                p.block.tensor(scores)

                def soft_v(v, i=i, tn=s_t.n):
                    v.wait_ge(s_t.h, tn)
                    sp = (i + 1) * 128
                    gs(v, v.tensor_tensor(out=ps_big[:, sp - 128:sp],
                                          in0=ps_big[:, sp - 128:sp],
                                          in1=msk[:, :], op=ALU.add))
                    gs(v, v.tensor_reduce(out=sgn[:, 0:1], in_=ps_big[:, 0:sp],
                                          axis=AX.X, op=ALU.max, negate=True))
                    gs(v, v.tensor_scalar(out=sgn[:, 1:2], in0=sgn[:, 0:1],
                                          scalar1=alpha, scalar2=None, op0=ALU.mult))
                p.block.vector(soft_v)

                def soft_s(s, i=i, vn=s_v.n):
                    s.wait_ge(s_v.h, vn)
                    sp = (i + 1) * 128
                    s_s.inc(s.activation(out=Pb[:, 0:sp], in_=ps_big[:, 0:sp],
                                         func=AF.Exp, bias=sgn[:, 1:2], scale=alpha,
                                         accum_out=sgn[:, 2:3]))
                p.block.scalar(soft_s)

                def soft_n(v, sn=s_s.n):
                    v.wait_ge(s_s.h, sn)
                    gs(v, v.reciprocal(out=sgn[:, 3:4], in_=sgn[:, 2:3]))
                p.block.vector(soft_n)
                V0 = s_v.n
                T0 = s_t.n

                def pvt(t, i=i, V0=V0):
                    last = None
                    for j in range(i + 1):
                        t.wait_ge(s_v.h, V0 + j)
                        ins = t.transpose(ps_tp[:, (j % 2) * 128:(j % 2 + 1) * 128],
                                          Pb[:, j * 128:(j + 1) * 128], ident[:, :])
                        ins.then_inc(s_t.h, 1)
                        s_t.n += 1
                        t.wait_ge(s_v.h, V0 + j + 1)
                        for lc in range(2):
                            last = t.matmul(
                                ps_lc[:, 256 + lc * 128:256 + (lc + 1) * 128],
                                ckvt[:, j, lc * 128:(lc + 1) * 128],
                                pT[:, j % 2, :], start=(j == 0), stop=(j == i))
                    s_t.inc(last)
                p.block.tensor(pvt)

                def pve(v, i=i, T0=T0):
                    for j in range(i + 1):
                        v.wait_ge(s_t.h, T0 + j + 1)
                        s_v.inc(v.tensor_copy(out=pT[:, j % 2, :],
                                              in_=ps_tp[:, (j % 2) * 128:(j % 2 + 1) * 128]))
                p.block.vector(pve)

                def evctx(v, tn=s_t.n):
                    v.wait_ge(s_t.h, tn)
                    s_v.inc(v.tensor_copy(out=ctxb[:, :], in_=ps_lc[:, 256:512]))
                p.block.vector(evctx)

                def mmy(t, h=h, vn=s_v.n):
                    t.wait_ge(s_v.h, vn)
                    for lc in range(2):
                        ins = t.matmul(ps_y[:, h * 64:(h + 1) * 64],
                                       ctxb[:, lc * 128:(lc + 1) * 128],
                                       Mv[:, lc, h * 64:(h + 1) * 64],
                                       start=(lc == 0), stop=(lc == 1))
                    s_t.inc(ins)
                p.block.tensor(mmy)

                def evy(v, h=h, tn=s_t.n, stn=s_st.n):
                    v.wait_ge(s_t.h, tn)
                    v.wait_ge(s_st.h, stn)
                    gs(v, v.tensor_scalar(out=ysb[:, h * 64:(h + 1) * 64],
                                          in0=ps_y[:, h * 64:(h + 1) * 64],
                                          scalar1=sgn[:, 3:4], scalar2=None,
                                          op0=ALU.mult))
                p.block.vector(evy)

            def ld_res(q, i=i, vn=s_v.n):
                q.wait_ge(s_v.h, vn)
                s_ld.inc(q.dma_start(out=xt[:, 0, :],
                                     in_=x_in[base + i * 128:base + (i + 1) * 128, :]), 16)
            p.block.sync(ld_res)
            ldn = s_ld.n

            def resid(v, ldn=ldn):
                v.wait_ge(s_ld.h, ldn)
                s_v.inc(v.tensor_tensor(out=ysb[:, :], in0=ysb[:, :],
                                        in1=xt[:, 0, :], op=ALU.add))
            p.block.vector(resid)
            vno = s_v.n

            def st_y(q, i=i, vno=vno):
                q.wait_ge(s_v.h, vno)
                s_st.inc(q.dma_start(out=x_out[base + i * 128:base + (i + 1) * 128, :],
                                     in_=ysb[:, :]), 16)
            p.block.sync(st_y)

    def fin(q):
        q.wait_ge(s_st.h, s_st.n)
    p.block.sync(fin)
    p.close()
    return p


def run_attn_b0(x_full, W_dq, W_uq, W_dkv, W_uk, W_uv, W_o):
    p = _get_prog("l0b", build_attn_b0)
    cf = const_feed(["ident", "ones_row", "iota128"])
    in_maps = []
    for core in range(NCORES):
        m = dict(cf)
        m["x"] = np.ascontiguousarray(
            x_full[core * ROWS:(core + 1) * ROWS].reshape(ROWS * T, D))
        m["W_dq"], m["W_uq"], m["W_dkv"] = W_dq, W_uq, W_dkv
        m["W_uk"], m["W_uv"], m["W_o"] = W_uk, W_uv, W_o
        in_maps.append(m)
    res = run_bass_kernel_spmd(p.nc, in_maps, core_ids=list(range(NCORES)))
    return np.stack([res.results[c]["y"].reshape(ROWS, T, D)
                     for c in range(NCORES)]).reshape(32, T, D)


def emit_uhg_ln(p, x_f32, scratch_f32, out_bf, j, stat, gb, gi, out_f32=None):
    sv, ss = p.sems["v"], p.sems["s"]
    n = D
    ALU_ = ALU

    def v1(v):
        v.wait_ge(sv.h, sv.n)
        sv.inc(v.tensor_scalar(out=scratch_f32[:, 0:n], in0=x_f32[:, :], scalar1=1.0,
                               scalar2=None, op0=ALU_.mult, op1=ALU_.add,
                               accum_out=stat[:, 0:1]))
        sv.inc(v.scalar_tensor_tensor(out=scratch_f32[:, 0:n], in0=x_f32[:, :],
                                      scalar=1.0, in1=x_f32[:, :], op0=ALU_.bypass,
                                      op1=ALU_.mult, accum_out=stat[:, 1:2]))
        v.wait_ge(sv.h, sv.n)
        sv.inc(v.tensor_scalar(out=stat[:, 2:3], in0=stat[:, 0:1], scalar1=1.0 / n,
                               scalar2=None, op0=ALU_.mult))
        sv.inc(v.tensor_scalar(out=stat[:, 3:4], in0=stat[:, 1:2], scalar1=1.0 / n,
                               scalar2=None, op0=ALU_.mult))
        v.wait_ge(sv.h, sv.n)
        sv.inc(v.tensor_tensor(out=stat[:, 5:6], in0=stat[:, 2:3], in1=stat[:, 2:3],
                               op=ALU_.mult))
        v.wait_ge(sv.h, sv.n)
        sv.inc(v.tensor_tensor(out=stat[:, 3:4], in0=stat[:, 3:4], in1=stat[:, 5:6],
                               op=ALU_.subtract))
        v.wait_ge(sv.h, sv.n)
        sv.inc(v.tensor_scalar(out=stat[:, 6:7], in0=stat[:, 3:4],
                               scalar1=float(n) / (n - 1), scalar2=1e-5,
                               op0=ALU_.mult, op1=ALU_.add))
    p.block.vector(v1)

    def s1(s):
        s.wait_ge(sv.h, sv.n)
        ss.inc(s.activation(out=stat[:, 7:8], in_=stat[:, 6:7], func=AF.Sqrt))
    p.block.scalar(s1)

    def v2(v):
        v.wait_ge(ss.h, ss.n)
        sv.inc(v.reciprocal(out=stat[:, 4:5], in_=stat[:, 7:8]))
        v.wait_ge(sv.h, sv.n)
        sv.inc(v.tensor_scalar(out=scratch_f32[:, :], in0=x_f32[:, :],
                               scalar1=stat[:, 2:3], scalar2=stat[:, 4:5],
                               op0=ALU_.subtract, op1=ALU_.mult))
        v.wait_ge(sv.h, sv.n)
        sv.inc(v.tensor_tensor(out=scratch_f32[:, :], in0=scratch_f32[:, :],
                               in1=gb[:, gi, :], op=ALU_.mult))
        v.wait_ge(sv.h, sv.n)
        if out_f32 is None:
            sv.inc(v.tensor_tensor(out=out_bf[:, j, :], in0=scratch_f32[:, :],
                                   in1=gb[:, gi + 1, :], op=ALU_.add))
        else:
            sv.inc(v.tensor_tensor(out=out_f32[:, :], in0=scratch_f32[:, :],
                                   in1=gb[:, gi + 1, :], op=ALU_.add))
    p.block.vector(v2)


_BUILD_CACHE = {}


def _get_prog(key, builder):
    if key not in _BUILD_CACHE:
        _BUILD_CACHE[key] = builder()
    return _BUILD_CACHE[key]


def run_attn_a(kind, x_full, Wq, bq, Wk, bk, Wv, bv, Wo, bo):
    p = _get_prog(kind, lambda: build_attn_a(kind))
    master = "alibi_master" if kind == "l0a" else "causal_master"
    cf = const_feed(["ident", master, "mask_far", "ones_row"])
    in_maps = []
    for core in range(NCORES):
        m = dict(cf)
        m["x"] = np.ascontiguousarray(
            x_full[core * ROWS:(core + 1) * ROWS].reshape(ROWS * T, D))
        m["Wq"], m["Wk"], m["Wv"], m["Wo"] = Wq, Wk, Wv, Wo
        m["bq"], m["bk"], m["bv"], m["bo"] = (b.reshape(1, D) for b in (bq, bk, bv, bo))
        in_maps.append(m)
    res = run_bass_kernel_spmd(p.nc, in_maps, core_ids=list(range(NCORES)))
    return np.stack([res.results[c]["y"].reshape(ROWS, T, D) for c in range(NCORES)]
                    ).reshape(32, T, D)


def run_ffn_l0(x_full, alpha, w_gate, b_gate, W1m, b1m, W2m, b2m, W1e, b1e, W2e, b2e):
    import os
    upto = os.environ.get("FFN_UPTO", "all")
    p = _get_prog("l0f" + upto, lambda: build_ffn_l0(upto))
    cf = const_feed(["ident", "ones_row", "ones_f32", "ones_col", "iota8", "iota128"])
    in_maps = []
    for core in range(NCORES):
        m = dict(cf)
        m["x"] = np.ascontiguousarray(
            x_full[core * ROWS:(core + 1) * ROWS].reshape(ROWS * T, D))
        m["alpha"] = np.asarray(alpha, np.float32).reshape(1, 1)
        m["w_gate"] = w_gate
        m["b_gate"] = b_gate.reshape(1, 8)
        m["W1m"], m["b1m"] = W1m, b1m.reshape(1, DFF)
        m["W2m"], m["b2m"] = W2m, b2m.reshape(1, D)
        m["W1e"] = np.ascontiguousarray(W1e.reshape(8 * D, DFF))
        m["b1e"] = b1e
        m["W2e"] = np.ascontiguousarray(W2e.reshape(8 * DFF, D))
        m["b2e"] = b2e
        in_maps.append(m)
    res = run_bass_kernel_spmd(p.nc, in_maps, core_ids=list(range(NCORES)))
    return np.stack([res.results[c]["y"].reshape(ROWS, T, D) for c in range(NCORES)]
                    ).reshape(32, T, D)


def run_ffn_l1(x_full, gate_w, w1, w2):
    p = _get_prog("l1f", build_ffn_l1)
    cf = const_feed(["ident", "ident_f32", "iota128"])
    in_maps = []
    for core in range(NCORES):
        m = dict(cf)
        m["x"] = np.ascontiguousarray(
            x_full[core * ROWS:(core + 1) * ROWS].reshape(ROWS * T, D))
        m["gate_w"] = gate_w
        m["w1"] = np.ascontiguousarray(w1.reshape(8 * D, DFF))
        m["w2"] = np.ascontiguousarray(w2.reshape(8 * DFF, D))
        in_maps.append(m)
    res = run_bass_kernel_spmd(p.nc, in_maps, core_ids=list(range(NCORES)))
    return np.stack([res.results[c]["y"].reshape(ROWS, T, D) for c in range(NCORES)]
                    ).reshape(32, T, D)


def run_attn_b1(x_full, wq, bq, wk, bk, wv, bv, wo, bo, w1, b1, w2, b2,
                g1, beta1, g2, beta2):
    p = _get_prog("l1b", build_attn_b1)
    cf = const_feed(["ident", "ones_row", "iota128", "rowpat"])
    in_maps = []
    for core in range(NCORES):
        m = dict(cf)
        m["x"] = np.ascontiguousarray(
            x_full[core * ROWS:(core + 1) * ROWS].reshape(ROWS * T, D))
        m["wq"], m["wk"], m["wv"], m["wo"] = wq, wk, wv, wo
        m["w1"], m["w2"] = w1, w2
        for n, v in (("bq", bq), ("bk", bk), ("bv", bv), ("bo", bo), ("b2", b2),
                     ("g1", g1), ("beta1", beta1), ("g2", g2), ("beta2", beta2)):
            m[n] = v.reshape(1, D)
        m["b1"] = b1.reshape(1, DFF)
        m["tstart"] = np.array([[core * (T // NCORES)]], np.float32)
        m["gbase"] = np.array([[core * ROWS]], np.float32)
        in_maps.append(m)
    res = run_bass_kernel_spmd(p.nc, in_maps, core_ids=list(range(NCORES)))
    return np.stack([res.results[c]["y"].reshape(ROWS, T, D) for c in range(NCORES)]
                    ).reshape(32, T, D)


def kernel(**inputs):
    d = inputs
    x = np.ascontiguousarray(np.asarray(d["L0_attn_a_x"], np.float32))
    x = run_attn_a("l0a", x,
                   d["L0_attn_a_Wq"], d["L0_attn_a_bq"],
                   d["L0_attn_a_Wk"], d["L0_attn_a_bk"],
                   d["L0_attn_a_Wv"], d["L0_attn_a_bv"],
                   d["L0_attn_a_Wo"], d["L0_attn_a_bo"])
    x = run_ffn_l0(x, d["L0_ffn_alpha"], d["L0_ffn_w_gate"], d["L0_ffn_b_gate"],
                   d["L0_ffn_W1m"], d["L0_ffn_b1m"], d["L0_ffn_W2m"], d["L0_ffn_b2m"],
                   d["L0_ffn_W1e"], d["L0_ffn_b1e"], d["L0_ffn_W2e"], d["L0_ffn_b2e"])
    x = run_attn_b0(x, d["L0_attn_b_W_dq"], d["L0_attn_b_W_uq"],
                    d["L0_attn_b_W_dkv"], d["L0_attn_b_W_uk"],
                    d["L0_attn_b_W_uv"], d["L0_attn_b_W_o"])
    x = run_ffn_l0(x, d["L0_ffn_alpha"], d["L0_ffn_w_gate"], d["L0_ffn_b_gate"],
                   d["L0_ffn_W1m"], d["L0_ffn_b1m"], d["L0_ffn_W2m"], d["L0_ffn_b2m"],
                   d["L0_ffn_W1e"], d["L0_ffn_b1e"], d["L0_ffn_W2e"], d["L0_ffn_b2e"])
    x = run_attn_a("l1a", x,
                   d["L1_attn_a_Wq"], d["L1_attn_a_bq"],
                   d["L1_attn_a_Wk"], d["L1_attn_a_bk"],
                   d["L1_attn_a_Wv"], d["L1_attn_a_bv"],
                   d["L1_attn_a_Wo"], d["L1_attn_a_bo"])
    x = run_ffn_l1(x, d["L1_ffn_gate_w"], d["L1_ffn_w1"], d["L1_ffn_w2"])
    x = run_attn_b1(x, d["L1_attn_b_wq"], d["L1_attn_b_bq"],
                    d["L1_attn_b_wk"], d["L1_attn_b_bk"],
                    d["L1_attn_b_wv"], d["L1_attn_b_bv"],
                    d["L1_attn_b_wo"], d["L1_attn_b_bo"],
                    d["L1_attn_b_w1"], d["L1_attn_b_b1"],
                    d["L1_attn_b_w2"], d["L1_attn_b_b2"],
                    d["L1_attn_b_g1"], d["L1_attn_b_beta1"],
                    d["L1_attn_b_g2"], d["L1_attn_b_beta2"])
    x = run_ffn_l1(x, d["L1_ffn_gate_w"], d["L1_ffn_w1"], d["L1_ffn_w2"])
    return np.ascontiguousarray(x.astype(np.float32))
